# version 29
# baseline (speedup 1.0000x reference)
import numpy as np
import ml_dtypes
import contextlib
import concourse.bass as bass
import concourse.mybir as mybir
from concourse.bass_utils import run_bass_kernel_spmd

F32 = mybir.dt.float32
BF16 = mybir.dt.bfloat16
AF = mybir.ActivationFunctionType
ALU = mybir.AluOpType
AX = mybir.AxisListType

L = 4096
DM = 4096
NROWS = 8192
NCORES = 8
RPC = NROWS // NCORES
RT = 256
NT = RPC // RT
NG = RT // 128
NCH = DM // 128
EPS = 1e-5


def _host_consts():
    f32 = np.float32
    bf16 = ml_dtypes.bfloat16
    cf = np.zeros((128, 529), dtype=f32)
    cf[:, 0:128] = np.eye(128, dtype=f32)
    cf[:, 128:256] = np.eye(128)[::-1]
    sg = np.ones((128, 128), dtype=f32)
    sg[0:64, 64:128] = -1.0
    sg[64:128, 0:64] = -1.0
    cf[:, 256:384] = sg
    cf[:, 384:385] = 1.0
    cf[0, 385:513] = 1.0
    k_, r_ = np.meshgrid(np.arange(16), np.arange(16), indexing="ij")
    cf[0:16, 513:529] = (k_ < r_).astype(f32)
    cb = np.zeros((128, 385), dtype=np.float32)
    cb[:, 0:128] = np.eye(128)
    cb[:, 128:256] = np.eye(128)[::-1]
    cb[:, 256:257] = 1.0
    for h in range(8):
        cb[h, 257 + 16 * h:257 + 16 * (h + 1)] = 1.0
    return {"c_f32a": cf, "c_bfa": cb.astype(bf16)}


_CONSTS = None


def _consts():
    global _CONSTS
    if _CONSTS is None:
        _CONSTS = _host_consts()
    return _CONSTS


def build_nc():
    nc = bass.Bass("TRN2", target_bir_lowering=False, debug=False)

    u = nc.dram_tensor("u", [RPC, DM], F32, kind="ExternalInput").ap()
    y = nc.dram_tensor("y", [RPC, DM], F32, kind="ExternalOutput").ap()
    kscr = nc.dram_tensor("kscr", [512], BF16, kind="Internal")

    din = {}
    for nm, shp, dt in [
        ("c_f32a", [128, 529], F32), ("c_bfa", [128, 385], BF16),
        ("w_A", [128, DM], F32), ("w_B", [128, DM], F32), ("w_Wr", [128, 256], F32),
        ("p_vec", [64, 137], F32), ("p_row", [1, 10], F32),
    ]:
        din[nm] = nc.dram_tensor(nm, shp, dt, kind="ExternalInput").ap()

    ex = contextlib.ExitStack()
    T = lambda name, shape, dt: ex.enter_context(nc.sbuf_tensor(name, shape, dt))

    with ex:
        x = [T("x0", [128, NG * DM], F32), T("x1", [128, NG * DM], F32)]
        xb = [T(f"xb{i}", [128, DM], BF16) for i in range(4)]
        xT = [T("xT0", [128, NCH * RT], BF16), T("xT1", [128, NCH * RT], BF16)]
        xTn = T("xTn", [128, NCH * RT], BF16)
        sqT = T("sqT", [128, NCH * RT // 2], BF16)
        a_bf = T("a_bf", [128, DM], BF16)
        b_bf = T("b_bf", [128, DM], BF16)
        wr_bf = T("wr_bf", [128, 256], BF16)
        cf32 = T("cf32", [128, 529], F32)
        cbfa = T("cbfa", [128, 385], BF16)
        pvec = T("pvec", [64, 137], F32)
        prow = T("prow", [1, 10], F32)
        sb = {
            "idn": cf32[:, 0:128], "flip32": cf32[:, 128:256], "sgn": cf32[:, 256:384],
            "ones128c": cf32[:, 384:385], "ones1x128": cf32[0:1, 385:513],
            "ones64": cf32[0:1, 385:449], "ut16": cf32[0:16, 513:529],
            "idnb": cbfa[:, 0:128], "flip": cbfa[:, 128:256],
            "onesb": cbfa[:, 256:257], "e8": cbfa[0:8, 257:385],
            "lamre": pvec[:, 0:1], "lamim": pvec[:, 1:2],
            "pre": pvec[:, 2:3], "pim": pvec[:, 3:4],
            "qre": pvec[:, 4:5], "qim": pvec[:, 5:6],
            "ctre": pvec[:, 6:7], "ctim": pvec[:, 7:8], "bp": pvec[:, 8:9],
            "vctre": pvec[:, 9:73], "vctim": pvec[:, 73:137],
            "dsc": prow[0:1, 0:1], "lst": prow[0:1, 1:2], "brv": prow[0:1, 2:10],
        }
        for nm, shp, dt in [
            ("stp", [1, 1], F32), ("sh", [1, 1], F32),
            ("sh64", [64, 1], F32), ("stp64", [64, 1], F32),
            ("bre", [64, 1], F32), ("bim", [64, 1], F32),
            ("wood", [64, 48], F32), ("cells", [1, 24], F32), ("rows6", [1, 384], F32),
            ("c2col", [128, 1], F32), ("c3col", [128, 1], F32),
            ("bprcol", [128, 1], F32), ("bprjcol", [128, 1], F32),
            ("kcol2", [128, 2], F32), ("t01cscol", [128, 2], F32),
            ("t01csrow", [1, 256], F32), ("drow", [1, 256], F32),
            ("klineb", [1, 512], BF16), ("hk", [128, 256], BF16),
            ("a128t", [128, 128], F32),
            ("p0_bfm", [128, 128], BF16), ("p1_bfm", [128, 128], BF16),
            ("e_bfm", [128, 256], BF16),
            ("t01_bfm", [128, 256], BF16), ("ronb", [128, 16], BF16),
            ("zf0", [128, RT], F32), ("zb0", [128, RT], BF16),
            ("zf1", [128, RT], F32), ("zb1", [128, RT], BF16),
            ("tddgt", [16, 256], BF16), ("tdd", [1, DM], BF16),
            ("anegrow", [1, 128], BF16), ("wrnegrow", [1, 8], BF16),
            ("brrep", [128, 8], F32), ("wk", [128, 2], F32),
            ("sumr", [1, RT], F32), ("sum2r", [1, RT], F32),
            ("wkr", [1, RT], F32), ("wkr2", [1, RT], F32),
            ("wkr30", [1, RT], F32), ("invsr0", [1, RT], F32), ("sc1r0", [1, RT], F32),
            ("negmb0", [1, RT], BF16), ("posmb0", [1, RT], BF16),
            ("invsc0", [128, NG], F32), ("sc1c0", [128, NG], F32), ("sc2c0", [128, NG], F32),
            ("wkr31", [1, RT], F32), ("invsr1", [1, RT], F32), ("sc1r1", [1, RT], F32),
            ("negmb1", [1, RT], BF16), ("posmb1", [1, RT], BF16),
            ("invsc1", [128, NG], F32), ("sc1c1", [128, NG], F32), ("sc2c1", [128, NG], F32),
            ("raw8", [8, RT], F32), ("rawt", [128, 8 * NG], F32),
            ("scrt", [128, 8 * NG], F32), ("smx", [128, NG], F32),
            ("ssum", [128, NG], F32), ("srec", [128, NG], F32),
            ("scfm", [8, RT], BF16), ("se", [128, RT], BF16),
            ("lows", [128, RT], BF16),
        ]:
            sb[nm] = T(nm, shp, dt)

        xTnf = xTn[:, :].bitcast(F32)

        def SLOT(k, w=128):
            return xTnf[:, 128 * k:128 * k + w]

        S_MROT = [0, 2]
        S_TROT = [1, 3]
        TKEEP = {1: 10, 2: 4, 4: 5, 8: 6, 16: 7, 32: 8, 64: 9,
                 128: 18, 512: 19, 1024: 20, 2048: 21}
        SGNM = {1: 11, 2: 12, 4: 13, 8: 14, 16: 15, 32: 16, 64: 17, 128: 22}
        S_AR, S_V, S_U, S_CM, S_CMT, S_ADG, S_VTH = 22, 23, 25, 27, 28, 29, 31

        astage = xT[0][:, :].bitcast(F32)[:, 0:4096]
        bstage = xT[1][:, :].bitcast(F32)[:, 0:4096]
        wrstage = xTnf[:, 128 * 30:128 * 30 + 256]

        pb = [ex.enter_context(nc.psum_tensor(f"pb{i}", [128, 512], F32)) for i in range(6)]

        sems = {}
        for nm in ["sg", "sp", "sa", "sv", "st", "sw0", "sw1", "sw2", "sk",
                   "si00", "si01", "si10", "si11"]:
            sems[nm] = [ex.enter_context(nc.semaphore(nm)), 0]

        def W(eng, s):
            eng.wait_ge(sems[s][0], sems[s][1])

        def WV(eng, s, val):
            if val > 0:
                eng.wait_ge(sems[s][0], val)

        def I(instr, s, n=1):
            instr.then_inc(sems[s][0], n)
            sems[s][1] += n
            return instr

        def M(s):
            return sems[s][1]

        block = ex.enter_context(nc.Block())
        SP, PL, PE, ACT, DVE = block.sync, block.gpsimd, block.tensor, block.scalar, block.vector
        MK2 = {}
        MK = {}
        bank_last = {b: 0 for b in range(6)}
        bank_last_a = {b: 0 for b in range(6)}

        def wb(t, *banks):
            for b in banks:
                WV(t, "sv", bank_last[b])
                WV(t, "sa", bank_last_a[b])

        def emit_tr(t_i, g, dst, cast_key, slot):
            for c in range(8):
                def pe_tr(t, g=g, c=c, slot=slot):
                    if c == 0:
                        WV(t, "sa", MK[cast_key])
                    wb(t, c % 2)
                    bk = pb[c % 2]
                    ii = None
                    for jj in range(4):
                        j = 4 * c + jj
                        ii = t.matmul(bk[:, 128 * jj:128 * (jj + 1)],
                                      xb[slot][:, 128 * j:128 * (j + 1)], sb["idnb"][:, :],
                                      start=True, stop=True)
                    I(ii, "sp")
                PE(pe_tr)
                mm = M("sp")

                def ev(e, g=g, c=c, mm=mm, t_i=t_i):
                    e.wait_ge(sems["sp"][0], mm)
                    if t_i < 2 and g == 0 and c == 0:
                        WV(e, "sa", MK2["castA" if t_i % 2 == 0 else "castB"])
                    bk = pb[c % 2]
                    dst_ap = dst[:, :].rearrange("p (j r) -> p j r", j=NCH)[:, 4 * c:4 * c + 4, 128 * g:128 * (g + 1)]
                    src_ap = bk[:, 0:512].rearrange("p (j r) -> p j r", j=4)
                    I(e.tensor_copy(dst_ap, src_ap), "sv")
                DVE(ev)
                bank_last[c % 2] = M("sv")
            MK[("slotfree", slot)] = M("sp")

        QW = NCH * RT // 4

        def emit_stats(t_i, src):
            def pe_sx(t):
                wb(t, 4, 5)
                ii = None
                for j in range(NCH):
                    ii = t.matmul(pb[4][0:1, 0:RT], sb["onesb"][:, :], src[:, RT * j:RT * j + RT],
                                  start=(j == 0), stop=(j == NCH - 1))
                I(ii, "sp")
            PE(pe_sx)
            qmarks = {}
            for q in range(4):
                def dve_sq(v, q=q):
                    if q >= 2:
                        WV(v, "sp", qmarks[q - 2])
                    h0 = (q % 2) * QW
                    I(v.tensor_mul(sqT[:, h0:h0 + QW], src[:, q * QW:(q + 1) * QW],
                                   src[:, q * QW:(q + 1) * QW]), "sv")
                DVE(dve_sq)
                MK[("sqq", q)] = M("sv")

                def pe_q(t, q=q):
                    WV(t, "sv", MK[("sqq", q)])
                    ii = None
                    for j in range(8):
                        ii = t.matmul(pb[5][0:1, 0:RT], sb["onesb"][:, :],
                                      sqT[:, (q % 2) * QW + RT * j:(q % 2) * QW + RT * j + RT],
                                      start=(q == 0 and j == 0), stop=(q == 3 and j == 7))
                    I(ii, "sp")
                PE(pe_q)
                qmarks[q] = M("sp")

        def emit_ln(kind, par):
            s_ = lambda nm: sb[nm + str(par)]

            def dve_lnA(v):
                W(v, "sp")
                v.tensor_copy(sb["sumr"][:, :], pb[4][0:1, 0:RT])
                v.tensor_copy(sb["sum2r"][:, :], pb[5][0:1, 0:RT])
                v.tensor_scalar(s_("negmb")[:, :], sb["sumr"][:, :], -1.0 / DM, None, ALU.mult)
                if kind == "ln2":
                    v.tensor_scalar(s_("posmb")[:, :], sb["sumr"][:, :], 1.0 / DM, None, ALU.mult)
                v.tensor_scalar(sb["wkr"][:, :], sb["sumr"][:, :], 1.0 / DM, None, ALU.mult)
                v.tensor_scalar(s_("wkr3")[:, :], sb["sum2r"][:, :], 1.0 / DM, None, ALU.mult)
                v.tensor_mul(sb["wkr2"][:, :], sb["wkr"][:, :], sb["wkr"][:, :])
                v.tensor_sub(s_("wkr3")[:, :], s_("wkr3")[:, :], sb["wkr2"][:, :])
                v.tensor_scalar(s_("wkr3")[:, :], s_("wkr3")[:, :], EPS, None, ALU.add)
                v.drain()
                I(v.reciprocal(s_("wkr3")[:, :], s_("wkr3")[:, :]), "sv")
            DVE(dve_lnA)
            bank_last[4] = M("sv")
            bank_last[5] = M("sv")
            MK[("negmb", par)] = M("sv")

            def act_ln(a):
                W(a, "sv")
                I(a.activation(s_("invsr")[:, :], s_("wkr3")[:, :], AF.Sqrt), "sa")
            ACT(act_ln)

            def dve_lnB(v):
                W(v, "sa")
                if kind == "ln1":
                    I(v.tensor_scalar(s_("sc1r")[:, :], s_("invsr")[:, :], sb["dsc"][:, :], 1.0, ALU.mult, ALU.add), "sv")
                else:
                    I(v.tensor_scalar(s_("sc1r")[:, :], s_("invsr")[:, :], 2.0, None, ALU.mult), "sv")
            DVE(dve_lnB)
            MK[("lnB", par)] = M("sv")

        def emit_lnT(kind, par):
            s_ = lambda nm: sb[nm + str(par)]

            def pe_ln(t):
                WV(t, "sv", MK[("lnB", par)])
                wb(t, 4)
                ii = None
                for g in range(NG):
                    t.matmul(pb[4][:, g:g + 1], s_("invsr")[0:1, 128 * g:128 * (g + 1)],
                             sb["ones1x128"][0:1, 0:1], start=True, stop=True)
                for g in range(NG):
                    ii = t.matmul(pb[4][:, NG + g:NG + g + 1], s_("sc1r")[0:1, 128 * g:128 * (g + 1)],
                                  sb["ones1x128"][0:1, 0:1], start=True, stop=True)
                I(ii, "sp")
            PE(pe_ln)
            lnmm = M("sp")

            def dve_lnC(v, lnmm=lnmm):
                v.wait_ge(sems["sp"][0], lnmm)
                v.tensor_copy(s_("invsc")[:, :], pb[4][:, 0:NG])
                I(v.tensor_copy(s_("sc1c" if kind == "ln1" else "sc2c")[:, :], pb[4][:, NG:2 * NG]), "sv")
            DVE(dve_lnC)
            bank_last[4] = M("sv")
            MK[("lnC", par)] = M("sv")

        def emit_build(t_i, g):
            p = t_i % 2
            slot = 2 * p + (g % 2)
            def act_cast(a, t_i=t_i, g=g, p=p, slot=slot):
                WV(a, *MK[("ld", t_i, g)])
                WV(a, "sp", MK.get(("slotfree", slot), 0))
                I(a.activation(xb[slot][:, :], x[p][:, DM * g:DM * (g + 1)], AF.Copy), "sa")
                MK[("cast", t_i, g)] = M("sa")
            ACT(act_cast)
            emit_tr(t_i, g, xT[p], ("cast", t_i, g), slot)

        st_marks = {}

        def emit_load(t_j):
            def sp_load(sp, t_j=t_j):
                if t_j >= 2:
                    WV(sp, "st", st_marks[t_j - 2])
                pq = t_j % 2
                for g in range(NG):
                    rows = t_j * RT + g * 128
                    snm = f"si{pq}{g}"
                    I(sp.dma_start(out=x[pq][:, DM * g:DM * (g + 1)], in_=u[rows:rows + 128, :]), snm, 16)
                    MK[("ld", t_j, g)] = (snm, M(snm))
            SP(sp_load)


        def pl_consts(g):
            g.memset(sb["klineb"][0:1, 0:127], 0.0)
            I(g.dma_start(out=pvec[:, :], in_=din["p_vec"]), "sg", 16)
            I(g.dma_start(out=prow[:, :], in_=din["p_row"]), "sg", 16)
            I(g.dma_start(out=cf32[:, :], in_=din["c_f32a"]), "sg", 16)
            I(g.dma_start(out=cbfa[:, :], in_=din["c_bfa"]), "sg", 16)
            MK2["params"] = M("sg")
            I(g.dma_start(out=astage, in_=din["w_A"]), "sw0", 16)
            I(g.dma_start(out=bstage, in_=din["w_B"]), "sw1", 16)
            I(g.dma_start(out=wrstage, in_=din["w_Wr"]), "sw2", 16)
        PL(pl_consts)

        def act_setup(a):
            WV(a, "sg", MK2["params"])
            I(a.activation(sb["stp"][:, :], sb["lst"][:, :], AF.Exp), "sa")
            MK2["stp"] = M("sa")
            WV(a, "sw0", 16)
            I(a.activation(a_bf[:, :], astage, AF.Copy), "sa")
            MK2["castA"] = M("sa")
        ACT(act_setup)

        def dve_sh(v):
            v.wait_ge(sems["sa"][0], MK2["stp"])
            I(v.tensor_scalar(sb["sh"][:, :], sb["stp"][:, :], 0.5, None, ALU.mult), "sv")
        DVE(dve_sh)

        emit_load(0)
        emit_load(1)

        def pe_setup1(t):
            W(t, "sv")
            wb(t, 0, 1)
            t.matmul(pb[0][0:64, 0:1], sb["ones64"][:, :], sb["sh"][:, :], start=True, stop=True)
            t.matmul(pb[0][0:64, 1:2], sb["ones64"][:, :], sb["stp"][:, :], start=True, stop=True)
            t.matmul(pb[0][:, 2:10], sb["ones1x128"][:, :], sb["brv"][:, :], start=True, stop=True)
            t.matmul(pb[0][0:1, 16:144], sb["dsc"][:, :], sb["ones1x128"][:, :], start=True, stop=True)
            t.matmul(pb[0][0:1, 144:272], sb["dsc"][:, :], sb["ones1x128"][:, :], start=True, stop=True)
            t.matmul(pb[1][0:64, 0:1], sb["vctre"][:, :], sb["bp"][:, :], start=True, stop=True)
            I(t.matmul(pb[1][0:64, 1:2], sb["vctim"][:, :], sb["bp"][:, :], start=True, stop=True), "sp")
        PE(pe_setup1)

        wd = sb["wood"]
        WC = lambda c: wd[:, c:c + 1]

        def dve_wood1(v):
            D_ = v.drain
            W(v, "sp")
            D_()
            v.tensor_copy(sb["sh64"][:, :], pb[0][0:64, 0:1])
            D_()
            v.tensor_copy(sb["stp64"][:, :], pb[0][0:64, 1:2])
            D_()
            v.tensor_copy(sb["brrep"][:, :], pb[0][:, 2:10])
            D_()
            v.tensor_copy(sb["drow"][:, :], pb[0][0:1, 16:272])
            D_()
            v.tensor_copy(sb["bre"][:, :], pb[1][0:64, 0:1])
            D_()
            v.tensor_copy(sb["bim"][:, :], pb[1][0:64, 1:2])
            D_()
            sh64 = sb["sh64"][:, :]
            D_()
            v.tensor_scalar(WC(0), sb["lamre"][:, :], sh64, None, ALU.mult)
            D_()
            v.tensor_scalar(WC(1), sb["lamim"][:, :], sh64, None, ALU.mult)
            D_()
            v.tensor_scalar(WC(12), sb["pre"][:, :], sh64, None, ALU.mult)
            D_()
            v.tensor_scalar(WC(13), sb["pim"][:, :], sh64, None, ALU.mult)
            D_()
            v.tensor_scalar(WC(14), sb["qim"][:, :], -1.0, None, ALU.mult)
            D_()
            D_()
            v.tensor_scalar(WC(2), WC(0), -1.0, 1.0, ALU.mult, ALU.add)
            D_()
            v.tensor_scalar(WC(3), WC(1), -1.0, None, ALU.mult)
            D_()
            v.tensor_scalar(WC(4), WC(0), 1.0, None, ALU.add)
            D_()
            v.tensor_copy(WC(5), WC(1))
            D_()
            D_()
            v.tensor_mul(WC(6), WC(2), WC(2))
            D_()
            v.tensor_mul(WC(44), WC(3), WC(3))
            D_()
            D_()
            v.tensor_add(WC(6), WC(6), WC(44))
            D_()
            D_()
            v.reciprocal(WC(7), WC(6))
            D_()
            D_()
            v.tensor_mul(WC(8), WC(2), WC(7))
            D_()
            v.tensor_mul(WC(44), WC(3), WC(7))
            D_()
            D_()
            v.tensor_scalar(WC(9), WC(44), -1.0, None, ALU.mult)
            D_()
            v.tensor_mul(WC(44), WC(4), WC(2))
            D_()
            v.tensor_mul(WC(45), WC(5), WC(3))
            D_()
            D_()
            v.tensor_add(WC(10), WC(44), WC(45))
            D_()
            v.tensor_mul(WC(44), WC(5), WC(2))
            D_()
            v.tensor_mul(WC(45), WC(4), WC(3))
            D_()
            D_()
            v.tensor_sub(WC(11), WC(44), WC(45))
            D_()
            D_()
            v.tensor_mul(WC(10), WC(10), WC(7))
            D_()
            v.tensor_mul(WC(11), WC(11), WC(7))
            D_()
            v.tensor_mul(WC(44), WC(8), WC(12))
            D_()
            v.tensor_mul(WC(45), WC(9), WC(13))
            D_()
            D_()
            v.tensor_sub(WC(15), WC(44), WC(45))
            D_()
            v.tensor_mul(WC(44), WC(8), WC(13))
            D_()
            v.tensor_mul(WC(45), WC(9), WC(12))
            D_()
            D_()
            v.tensor_add(WC(16), WC(44), WC(45))
            D_()
            v.tensor_mul(WC(44), sb["qre"][:, :], WC(8))
            D_()
            v.tensor_mul(WC(45), WC(14), WC(9))
            D_()
            D_()
            v.tensor_sub(WC(17), WC(44), WC(45))
            D_()
            v.tensor_mul(WC(44), sb["qre"][:, :], WC(9))
            D_()
            v.tensor_mul(WC(45), WC(14), WC(8))
            D_()
            D_()
            v.tensor_add(WC(18), WC(44), WC(45))
            D_()
            D_()
            v.tensor_mul(WC(44), WC(17), WC(12))
            D_()
            v.tensor_mul(WC(45), WC(18), WC(13))
            D_()
            D_()
            v.tensor_sub(WC(19), WC(44), WC(45))
            D_()
            v.tensor_mul(WC(44), WC(17), WC(13))
            D_()
            v.tensor_mul(WC(45), WC(18), WC(12))
            D_()
            D_()
            v.tensor_add(WC(20), WC(44), WC(45))
            D_()
            v.tensor_mul(WC(44), WC(17), sb["bre"][:, :])
            D_()
            v.tensor_mul(WC(45), WC(18), sb["bim"][:, :])
            D_()
            D_()
            v.tensor_sub(WC(21), WC(44), WC(45))
            D_()
            v.tensor_mul(WC(44), WC(17), sb["bim"][:, :])
            D_()
            v.tensor_mul(WC(45), WC(18), sb["bre"][:, :])
            D_()
            D_()
            v.tensor_add(WC(22), WC(44), WC(45))
            D_()
            v.tensor_mul(WC(44), WC(17), WC(4))
            D_()
            v.tensor_mul(WC(45), WC(18), WC(5))
            D_()
            D_()
            v.tensor_sub(WC(23), WC(44), WC(45))
            D_()
            v.tensor_mul(WC(44), WC(17), WC(5))
            D_()
            v.tensor_mul(WC(45), WC(18), WC(4))
            D_()
            D_()
            I(v.tensor_add(WC(24), WC(44), WC(45)), "sv")
        DVE(dve_wood1)
        bank_last[0] = M("sv")
        bank_last[1] = M("sv")
        MK2["wd1"] = M("sv")

        def pe_reduce(t):
            WV(t, "sv", MK2["wd1"])
            I(t.matmul(pb[2][0:1, 0:4], sb["ones128c"][0:64, 0:1], wd[:, 19:23], start=True, stop=True), "sp")
        PE(pe_reduce)
        MK2["red"] = M("sp")

        ce = sb["cells"]
        CC = lambda c: ce[0:1, c:c + 1]

        def dve_wood2(v):
            D_ = v.drain
            v.wait_ge(sems["sp"][0], MK2["red"])
            D_()
            v.tensor_copy(ce[0:1, 0:4], pb[2][0:1, 0:4])
            D_()
            D_()
            v.tensor_scalar(CC(4), CC(0), 1.0, None, ALU.add)
            D_()
            D_()
            v.tensor_mul(CC(14), CC(4), CC(4))
            D_()
            v.tensor_mul(CC(15), CC(1), CC(1))
            D_()
            D_()
            v.tensor_add(CC(5), CC(14), CC(15))
            D_()
            D_()
            v.reciprocal(CC(16), CC(5))
            D_()
            D_()
            v.tensor_mul(CC(6), CC(4), CC(16))
            D_()
            v.tensor_mul(CC(14), CC(1), CC(16))
            D_()
            D_()
            v.tensor_scalar(CC(7), CC(14), -1.0, None, ALU.mult)
            D_()
            D_()
            v.tensor_mul(CC(14), CC(0), CC(6))
            D_()
            v.tensor_mul(CC(15), CC(1), CC(7))
            D_()
            D_()
            v.tensor_sub(CC(8), CC(14), CC(15))
            D_()
            D_()
            v.tensor_mul(CC(14), CC(0), CC(7))
            D_()
            v.tensor_mul(CC(15), CC(1), CC(6))
            D_()
            D_()
            v.tensor_add(CC(9), CC(14), CC(15))
            D_()
            v.tensor_scalar(CC(8), CC(8), 1.0, None, ALU.subtract)
            D_()
            v.tensor_scalar(CC(10), CC(6), -1.0, None, ALU.mult)
            D_()
            v.tensor_scalar(CC(11), CC(7), -1.0, None, ALU.mult)
            D_()
            v.tensor_mul(CC(14), CC(2), CC(6))
            D_()
            v.tensor_mul(CC(15), CC(3), CC(7))
            D_()
            D_()
            v.tensor_sub(CC(12), CC(14), CC(15))
            D_()
            D_()
            v.tensor_mul(CC(14), CC(2), CC(7))
            D_()
            v.tensor_mul(CC(15), CC(3), CC(6))
            D_()
            D_()
            I(v.tensor_add(CC(13), CC(14), CC(15)), "sv")
        DVE(dve_wood2)
        bank_last[2] = M("sv")
        MK2["wd2"] = M("sv")

        def pe_bcast(t):
            WV(t, "sv", MK2["wd2"])
            wb(t, 3)
            I(t.matmul(pb[3][0:64, 0:6], sb["ones64"][:, :], ce[0:1, 8:14], start=True, stop=True), "sp")
        PE(pe_bcast)
        MK2["bc"] = M("sp")

        def dve_wood3(v):
            D_ = v.drain
            v.wait_ge(sems["sp"][0], MK2["bc"])
            D_()
            v.tensor_copy(wd[:, 25:31], pb[3][0:64, 0:6])
            D_()
            D_()
            v.tensor_mul(WC(44), sb["qre"][:, :], WC(25))
            D_()
            v.tensor_mul(WC(45), WC(14), WC(26))
            D_()
            D_()
            v.tensor_sub(WC(31), WC(44), WC(45))
            D_()
            v.tensor_mul(WC(44), WC(23), WC(27))
            D_()
            v.tensor_mul(WC(45), WC(24), WC(28))
            D_()
            D_()
            v.tensor_sub(WC(40), WC(44), WC(45))
            D_()
            D_()
            v.tensor_add(WC(31), WC(31), WC(40))
            D_()
            v.tensor_mul(WC(44), sb["qre"][:, :], WC(26))
            D_()
            v.tensor_mul(WC(45), WC(14), WC(25))
            D_()
            D_()
            v.tensor_add(WC(32), WC(44), WC(45))
            D_()
            v.tensor_mul(WC(44), WC(23), WC(28))
            D_()
            v.tensor_mul(WC(45), WC(24), WC(27))
            D_()
            D_()
            v.tensor_add(WC(40), WC(44), WC(45))
            D_()
            D_()
            v.tensor_add(WC(32), WC(32), WC(40))
            D_()
            v.tensor_mul(WC(44), WC(8), sb["bre"][:, :])
            D_()
            v.tensor_mul(WC(45), WC(9), sb["bim"][:, :])
            D_()
            D_()
            v.tensor_sub(WC(33), WC(44), WC(45))
            D_()
            v.tensor_mul(WC(44), WC(8), sb["bim"][:, :])
            D_()
            v.tensor_mul(WC(45), WC(9), sb["bre"][:, :])
            D_()
            D_()
            v.tensor_add(WC(34), WC(44), WC(45))
            D_()
            v.tensor_mul(WC(44), WC(29), WC(15))
            D_()
            v.tensor_mul(WC(45), WC(30), WC(16))
            D_()
            D_()
            v.tensor_sub(WC(40), WC(44), WC(45))
            D_()
            D_()
            v.tensor_sub(WC(33), WC(33), WC(40))
            D_()
            v.tensor_mul(WC(44), WC(29), WC(16))
            D_()
            v.tensor_mul(WC(45), WC(30), WC(15))
            D_()
            D_()
            v.tensor_add(WC(40), WC(44), WC(45))
            D_()
            D_()
            v.tensor_sub(WC(34), WC(34), WC(40))
            D_()
            D_()
            v.tensor_scalar(WC(33), WC(33), sb["stp64"][:, :], None, ALU.mult)
            D_()
            v.tensor_scalar(WC(34), WC(34), sb["stp64"][:, :], None, ALU.mult)
            D_()
            v.tensor_scalar(WC(35), WC(15), -1.0, None, ALU.mult)
            D_()
            v.tensor_scalar(WC(36), WC(16), -1.0, None, ALU.mult)
            D_()
            v.tensor_scalar(WC(37), WC(11), -1.0, None, ALU.mult)
            D_()
            D_()
            v.tensor_copy(WC(38), WC(15))
            D_()
            v.tensor_copy(WC(39), WC(16))
            D_()
            v.tensor_copy(WC(40), WC(35))
            D_()
            v.tensor_copy(WC(41), WC(36))
            D_()
            v.tensor_copy(WC(42), WC(31))
            D_()
            v.tensor_copy(WC(43), WC(32))
            D_()
            D_()
            v.tensor_copy(sb["bprcol"][0:64, :], WC(33))
            D_()
            v.tensor_copy(sb["bprcol"][64:128, :], WC(34))
            D_()
            v.tensor_copy(sb["c2col"][0:64, :], sb["ctre"][:, :])
            D_()
            v.tensor_scalar(sb["c2col"][64:128, :], sb["ctim"][:, :], -1.0, None, ALU.mult)
            D_()
            v.tensor_copy(sb["c3col"][0:64, :], sb["ctre"][:, :])
            D_()
            v.tensor_copy(sb["c3col"][64:128, :], sb["ctim"][:, :])
            idq = sb["idn"][0:64, 0:64]
            adg = SLOT(S_ADG)
            D_()
            v.tensor_scalar(adg[0:64, 0:64], idq, WC(10), None, ALU.mult)
            D_()
            v.tensor_scalar(adg[0:64, 64:128], idq, WC(37), None, ALU.mult)
            D_()
            v.tensor_scalar(adg[64:128, 0:64], idq, WC(11), None, ALU.mult)
            D_()
            I(v.tensor_scalar(adg[64:128, 64:128], idq, WC(10), None, ALU.mult), "sv")
        DVE(dve_wood3)
        bank_last[3] = M("sv")
        MK2["wd3"] = M("sv")

        def pe_rows(t):
            WV(t, "sv", MK2["wd3"])
            wb(t, 2)
            ii = None
            for k, c in enumerate([38, 39, 40, 41, 42, 43]):
                ii = t.transpose(pb[2][0:1, 64 * k:64 * k + 64], wd[:, c:c + 1], sb["idn"][0:64, 0:64])
            I(ii, "sp")
        PE(pe_rows)
        MK2["rows"] = M("sp")

        def dve_rows(v):
            v.wait_ge(sems["sp"][0], MK2["rows"])
            I(v.tensor_copy(sb["rows6"][:, :], pb[2][0:1, 0:384]), "sv")
        DVE(dve_rows)
        bank_last[2] = M("sv")
        MK2["rows_v"] = M("sv")

        R6 = lambda k: sb["rows6"][0:1, 64 * k:64 * k + 64]

        def pe_arq(t):
            WV(t, "sv", MK2["rows_v"])
            wb(t, 3)
            t.matmul(pb[3][:, 0:128], sb["idn"][:, :], SLOT(S_ADG), start=True, stop=False)
            quads = [
                (0, 0, [(0, 4), (3, 5)], False),
                (0, 64, [(3, 4), (2, 5)], True),
                (64, 0, [(1, 4), (0, 5)], False),
                (64, 64, [(0, 4), (3, 5)], True),
            ]
            ii = None
            for pa, cb, terms, fin in quads:
                for j, (lk, rk) in enumerate(terms):
                    ii = t.matmul(pb[3][pa:pa + 64, cb:cb + 64], R6(lk), R6(rk),
                                  start=False, stop=(fin and j == 1))
            I(ii, "sp")
        PE(pe_arq)
        MK2["arq"] = M("sp")

        def dve_arq(v):
            v.wait_ge(sems["sp"][0], MK2["arq"])
            v.tensor_copy(SLOT(S_AR), pb[3][:, 0:128])
            v.drain()
            I(v.tensor_mul(SLOT(SGNM[1]), sb["sgn"][:, :], SLOT(S_AR)), "sv")
        DVE(dve_arq)
        bank_last[3] = M("sv")
        MK2["ar_v"] = M("sv")

        def pe_art(t):
            WV(t, "sv", MK2["ar_v"])
            wb(t, 2)
            I(t.transpose(pb[2][:, 0:128], SLOT(S_AR), sb["idn"][:, :]), "sp")
        PE(pe_art)
        MK2["art"] = M("sp")

        def dve_art(v):
            v.wait_ge(sems["sp"][0], MK2["art"])
            I(v.tensor_copy(SLOT(TKEEP[1]), pb[2][:, 0:128]), "sv")
        DVE(dve_art)
        bank_last[2] = M("sv")

        emit_build(0, 0)

        def act_setup2(a):
            WV(a, "sw1", 16)
            I(a.activation(b_bf[:, :], bstage, AF.Copy), "sa")
            MK2["castB"] = M("sa")
            WV(a, "sw2", 16)
            I(a.activation(wr_bf[:, :], wrstage, AF.Copy), "sa")
        ACT(act_setup2)
        MK[("setup_sa",)] = M("sa")

        sb_rot = [0]

        def setup_bank():
            b = 2 + (sb_rot[0] % 2)
            sb_rot[0] += 1
            return b

        chain_state = {"M": S_AR, "T": TKEEP[1]}

        def emit_chain(lev):
            w2 = 2 << lev
            Msrc, Tsrc = chain_state["M"], chain_state["T"]
            do_sq = lev <= 20
            Mdst = S_MROT[lev % 2]
            if w2 == 256:
                Tslot = None
                Tdst_ap = sb["a128t"][:, :]
            else:
                Tslot = TKEEP.get(w2, S_TROT[lev % 2])
                Tdst_ap = SLOT(Tslot)
            b = setup_bank()

            def pe_ch(t, lev=lev, Msrc=Msrc, Tsrc=Tsrc, b=b, do_sq=do_sq):
                W(t, "sv")
                if do_sq:
                    t.matmul(pb[b][:, 0:128], SLOT(Tsrc) if Tsrc != "a128t" else sb["a128t"][:, :],
                             SLOT(Msrc), start=True, stop=True)
                    ii = t.matmul(pb[b][:, 128:256], SLOT(Msrc),
                                  SLOT(Tsrc) if Tsrc != "a128t" else sb["a128t"][:, :],
                                  start=True, stop=True)
                if lev >= 12:
                    t.matmul(pb[b][:, 256:257],
                             SLOT(Tsrc) if Tsrc != "a128t" else sb["a128t"][:, :],
                             sb["bprcol"][:, :], start=True, stop=False)
                    ii = t.matmul(pb[b][:, 256:257], sb["idn"][:, :], sb["bprcol"][:, :],
                                  start=False, stop=True)
                I(ii, "sp")
            PE(pe_ch)
            mm = M("sp")

            def dve_ch(v, lev=lev, b=b, Mdst=Mdst, Tdst_ap=Tdst_ap, w2=w2, mm=mm, do_sq=do_sq):
                v.wait_ge(sems["sp"][0], mm)
                ops = []
                if do_sq:
                    ops.append((SLOT(Mdst), pb[b][:, 0:128]))
                    ops.append((Tdst_ap, pb[b][:, 128:256]))
                if lev >= 12:
                    ops.append((sb["bprcol"][:, :], pb[b][:, 256:257]))
                if do_sq and w2 in SGNM:
                    for d, s in ops:
                        v.tensor_copy(d, s)
                    v.drain()
                    I(v.tensor_mul(SLOT(SGNM[w2]), sb["sgn"][:, :], SLOT(Mdst)), "sv")
                else:
                    for d, s in ops[:-1]:
                        v.tensor_copy(d, s)
                    I(v.tensor_copy(*ops[-1]), "sv")
            DVE(dve_ch)
            bank_last[b] = M("sv")
            chain_state["M"] = Mdst
            chain_state["T"] = "a128t" if w2 == 256 else Tslot

        def emit_u0():
            b = setup_bank()

            def pe_u(t, b=b):
                W(t, "sv")
                wb(t, b)
                I(t.matmul(pb[b][:, 0:1], SLOT(SGNM[1]), sb["c2col"][:, :], start=True, stop=True), "sp")
            PE(pe_u)
            mm = M("sp")

            def dve_u(v, b=b, mm=mm):
                v.wait_ge(sems["sp"][0], mm)
                I(v.tensor_copy(SLOT(S_U, 256)[:, 0:1], pb[b][:, 0:1]), "sv")
            DVE(dve_u)
            bank_last[b] = M("sv")

        def emit_udbl(w):
            b = setup_bank()

            def pe_u(t, b=b, w=w):
                W(t, "sv")
                wb(t, b)
                I(t.matmul(pb[b][:, 0:w], SLOT(SGNM[w]), SLOT(S_U, 256)[:, 0:w], start=True, stop=True), "sp")
            PE(pe_u)
            mm = M("sp")

            def dve_u(v, b=b, w=w, mm=mm):
                v.wait_ge(sems["sp"][0], mm)
                I(v.tensor_copy(SLOT(S_U, 256)[:, w:2 * w], pb[b][:, 0:w]), "sv")
            DVE(dve_u)
            bank_last[b] = M("sv")

        emit_chain(0); emit_chain(1); emit_chain(2)
        emit_build(0, 1)
        emit_chain(3); emit_chain(4); emit_chain(5)
        emit_build(1, 0)
        emit_chain(6)
        emit_u0()
        emit_chain(7)
        emit_udbl(1)
        emit_stats(0, xT[0])
        emit_chain(8)
        emit_udbl(2)
        emit_chain(9)
        emit_udbl(4)
        emit_ln("ln1", 0)
        emit_chain(10)
        emit_udbl(8)
        emit_chain(11)
        emit_udbl(16)
        emit_build(1, 1)
        emit_chain(12)
        emit_udbl(32)
        emit_chain(13)
        emit_udbl(64)
        emit_chain(14)
        emit_udbl(128)
        emit_stats(1, xT[1])
        emit_chain(15); emit_chain(16); emit_chain(17)
        emit_ln("ln1", 1)
        emit_chain(18); emit_chain(19); emit_chain(20); emit_chain(21)

        def dve_ecast(v):
            W(v, "sv")
            v.tensor_copy(sb["e_bfm"][0:64, :], SLOT(S_U, 256)[0:64, :])
            v.tensor_scalar(sb["e_bfm"][64:128, :], SLOT(S_U, 256)[64:128, :], -1.0, None, ALU.mult)
            v.tensor_copy(sb["bprjcol"][0:64, :], sb["bprcol"][0:64, :])
            v.tensor_scalar(sb["bprjcol"][64:128, :], sb["bprcol"][64:128, :], -1.0, None, ALU.mult)
            I(v.tensor_copy(SLOT(S_V, 256)[:, 0:1], sb["bprcol"][:, :]), "sv")
        DVE(dve_ecast)
        MK2["ecast"] = M("sv")

        for w in [1, 2, 4, 8, 16, 32, 64, 128]:
            b = setup_bank()

            def pe_v(t, b=b, w=w):
                W(t, "sv")
                wb(t, b)
                I(t.matmul(pb[b][:, 0:w], SLOT(TKEEP[w]), SLOT(S_V, 256)[:, 0:w], start=True, stop=True), "sp")
            PE(pe_v)
            mm = M("sp")

            def dve_v(v, b=b, w=w, mm=mm):
                v.wait_ge(sems["sp"][0], mm)
                I(v.tensor_copy(SLOT(S_V, 256)[:, w:2 * w], pb[b][:, 0:w]), "sv")
            DVE(dve_v)
            bank_last[b] = M("sv")

        for h, pdst in ((0, "p1_bfm"), (1, "p0_bfm")):
            b_vt = setup_bank()

            def pe_vt(t, h=h, b_vt=b_vt):
                W(t, "sv")
                wb(t, b_vt)
                I(t.transpose(pb[b_vt][:, 0:128], SLOT(S_V, 256)[:, 128 * h:128 * h + 128], sb["idn"][:, :]), "sp")
            PE(pe_vt)
            mm_vt = M("sp")

            def dve_vt(v, b_vt=b_vt, mm_vt=mm_vt):
                v.wait_ge(sems["sp"][0], mm_vt)
                WV(v, "sa", MK[("setup_sa",)])
                I(v.tensor_copy(SLOT(S_VTH), pb[b_vt][:, 0:128]), "sv")
            DVE(dve_vt)
            bank_last[b_vt] = M("sv")

            b_p = setup_bank()

            def pe_p(t, b_p=b_p):
                W(t, "sv")
                wb(t, b_p)
                I(t.matmul(pb[b_p][:, 0:128], sb["flip32"][:, :], SLOT(S_VTH), start=True, stop=True), "sp")
            PE(pe_p)
            mm_p = M("sp")

            def dve_p(v, b_p=b_p, pdst=pdst, mm_p=mm_p):
                v.wait_ge(sems["sp"][0], mm_p)
                I(v.tensor_copy(sb[pdst][:, :], pb[b_p][:, 0:128]), "sv")
            DVE(dve_p)
            bank_last[b_p] = M("sv")

        b_k = setup_bank()

        def pe_k(t):
            W(t, "sv")
            wb(t, b_k)
            t.matmul(pb[b_k][:, 0:1], SLOT(S_U, 256)[:, 0:128], sb["bprjcol"][:, :], start=True, stop=True)
            t.matmul(pb[b_k][:, 1:2], SLOT(S_U, 256)[:, 128:256], sb["bprjcol"][:, :], start=True, stop=True)
            I(t.matmul(pb[b_k][0:1, 2:3], sb["c3col"][:, :], sb["bprcol"][:, :], start=True, stop=True), "sp")
        PE(pe_k)
        MK2["kk"] = M("sp")

        def dve_k(v):
            v.wait_ge(sems["sp"][0], MK2["kk"])
            v.tensor_copy(sb["kcol2"][:, :], pb[b_k][:, 0:2])
            I(v.tensor_copy(sb["klineb"][0:1, 127:128], pb[b_k][0:1, 2:3]), "sv")
        DVE(dve_k)
        MK2["kv"] = M("sv")

        b_k2 = setup_bank()

        def pe_k2(t):
            WV(t, "sv", MK2["kv"])
            wb(t, b_k2)
            t.matmul(pb[b_k2][0:1, 0:128], sb["kcol2"][:, 0:1], sb["idn"][:, :], start=True, stop=True)
            I(t.matmul(pb[b_k2][0:1, 128:256], sb["kcol2"][:, 1:2], sb["idn"][:, :], start=True, stop=True), "sp")
        PE(pe_k2)
        MK2["k2"] = M("sp")

        def dve_k2(v):
            v.wait_ge(sems["sp"][0], MK2["k2"])
            I(v.tensor_copy(sb["klineb"][0:1, 128:384], pb[b_k2][0:1, 0:256]), "sv")
        DVE(dve_k2)
        bank_last[b_k] = M("sv")
        bank_last[b_k2] = M("sv")
        MK2["kline"] = M("sv")

        def pl_kscr(g):
            WV(g, "sv", MK2["kline"])
            dst = kscr.ap()[0:512].rearrange("(p f) -> p f", p=1)
            I(g.dma_start(out=dst, in_=sb["klineb"][:, :]), "sk", 16)
        PL(pl_kscr)

        def pl_hk(g):
            WV(g, "sk", 16)
            src = bass.AP(tensor=kscr.ap().tensor, offset=0, ap=[[1, 128], [1, 256]])
            I(g.dma_start(out=sb["hk"][:, :], in_=src), "sk", 16)
        PL(pl_hk)

        b_t0 = setup_bank()

        def pe_t0(t):
            WV(t, "sk", 32)
            wb(t, b_t0)
            I(t.matmul(pb[b_t0][:, 0:256], sb["flip"][:, :], sb["hk"][:, :], start=True, stop=True), "sp")
        PE(pe_t0)
        MK2["t0"] = M("sp")

        def dve_t0(v):
            v.wait_ge(sems["sp"][0], MK2["t0"])
            I(v.tensor_copy(sb["t01_bfm"][:, :], pb[b_t0][:, 0:256]), "sv")
        DVE(dve_t0)
        bank_last[b_t0] = M("sv")

        b_s1 = setup_bank()

        def pe_s1(t):
            W(t, "sv")
            wb(t, b_s1)
            t.matmul(pb[b_s1][:, 0:1], sb["p0_bfm"][:, :], sb["onesb"][:, :], start=True, stop=False)
            I(t.matmul(pb[b_s1][:, 0:1], sb["p1_bfm"][:, :], sb["onesb"][:, :], start=False, stop=True), "sp")
        PE(pe_s1)
        mm_s1 = M("sp")

        def dve_s1(v):
            v.wait_ge(sems["sp"][0], mm_s1)
            I(v.tensor_copy(SLOT(S_CM)[:, 0:1], pb[b_s1][:, 0:1]), "sv")
        DVE(dve_s1)
        bank_last[b_s1] = M("sv")

        for w in [1, 2, 4, 8]:
            b = setup_bank()

            def pe_cm(t, b=b, w=w):
                W(t, "sv")
                wb(t, b)
                lhs = sb["a128t"][:, :] if w == 1 else SLOT(TKEEP[256 * w])
                I(t.matmul(pb[b][:, 0:w], lhs, SLOT(S_CM)[:, 0:w], start=True, stop=True), "sp")
            PE(pe_cm)
            mm = M("sp")

            def dve_cm(v, b=b, w=w, mm=mm):
                v.wait_ge(sems["sp"][0], mm)
                I(v.tensor_copy(SLOT(S_CM)[:, w:2 * w], pb[b][:, 0:w]), "sv")
            DVE(dve_cm)
            bank_last[b] = M("sv")

        b_cmt = setup_bank()

        def pe_cmt(t):
            W(t, "sv")
            wb(t, b_cmt)
            I(t.transpose(pb[b_cmt][0:16, 0:128], SLOT(S_CM)[:, 0:16], sb["idn"][:, :]), "sp")
        PE(pe_cmt)
        MK2["cmt"] = M("sp")

        def dve_cmt(v):
            v.wait_ge(sems["sp"][0], MK2["cmt"])
            I(v.tensor_copy(SLOT(S_CMT)[0:16, 0:128], pb[b_cmt][0:16, 0:128]), "sv")
        DVE(dve_cmt)
        bank_last[b_cmt] = M("sv")

        b_ron = setup_bank()

        def pe_ron(t):
            W(t, "sv")
            wb(t, b_ron)
            I(t.matmul(pb[b_ron][:, 0:16], SLOT(S_CMT)[0:16, 0:128], sb["ut16"][:, :], start=True, stop=True), "sp")
        PE(pe_ron)
        MK2["ron"] = M("sp")

        def dve_ron(v):
            v.wait_ge(sems["sp"][0], MK2["ron"])
            I(v.tensor_copy(sb["ronb"][:, :], pb[b_ron][:, 0:16]), "sv")
        DVE(dve_ron)
        bank_last[b_ron] = M("sv")

        b_tc = setup_bank()

        def pe_t0cs(t):
            W(t, "sv")
            wb(t, b_tc)
            t.matmul(pb[b_tc][:, 0:1], sb["t01_bfm"][:, 0:128], sb["onesb"][:, :], start=True, stop=True)
            I(t.matmul(pb[b_tc][:, 1:2], sb["t01_bfm"][:, 128:256], sb["onesb"][:, :], start=True, stop=True), "sp")
        PE(pe_t0cs)
        mm_tc = M("sp")

        def dve_t0cs(v):
            v.wait_ge(sems["sp"][0], mm_tc)
            I(v.tensor_copy(sb["t01cscol"][:, :], pb[b_tc][:, 0:2]), "sv")
        DVE(dve_t0cs)
        bank_last[b_tc] = M("sv")
        MK2["t0cs"] = M("sv")

        b_tcr = setup_bank()

        def pe_t0csr(t):
            WV(t, "sv", MK2["t0cs"])
            wb(t, b_tcr)
            t.matmul(pb[b_tcr][0:1, 0:128], sb["t01cscol"][:, 0:1], sb["idn"][:, :], start=True, stop=True)
            I(t.matmul(pb[b_tcr][0:1, 128:256], sb["t01cscol"][:, 1:2], sb["idn"][:, :], start=True, stop=True), "sp")
        PE(pe_t0csr)
        mm_tcr = M("sp")

        def dve_t0csr(v):
            v.wait_ge(sems["sp"][0], mm_tcr)
            I(v.tensor_copy(sb["t01csrow"][:, :], pb[b_tcr][0:1, 0:256]), "sv")
        DVE(dve_t0csr)
        bank_last[b_tcr] = M("sv")

        b_tx = setup_bank()

        def pe_tcross(t):
            W(t, "sv")
            wb(t, b_tx)
            o16 = sb["ones1x128"][0:1, 0:16]
            t.matmul(pb[b_tx][0:16, 0:256], sb["ronb"][:, :], sb["e_bfm"][:, :], start=True, stop=False)
            t.matmul(pb[b_tx][0:16, 0:256], o16, sb["t01csrow"][:, :], start=False, stop=False)
            t.matmul(pb[b_tx][0:16, 128:256], o16, sb["t01csrow"][0:1, 0:128], start=False, stop=False)
            I(t.matmul(pb[b_tx][0:16, 0:256], o16, sb["drow"][:, :], start=False, stop=True), "sp")
        PE(pe_tcross)
        MK2["tx"] = M("sp")

        def dve_tcross(v):
            v.wait_ge(sems["sp"][0], MK2["tx"])
            I(v.tensor_copy(sb["tddgt"][:, :], pb[b_tx][0:16, 0:256]), "sv")
        DVE(dve_tcross)
        bank_last[b_tx] = M("sv")

        def pl_tdd(g):
            W(g, "sv")
            dst = sb["tdd"][0:1, :].rearrange("p (j c) -> p j c", j=16)
            I(g.dma_start(out=dst, in_=sb["tddgt"][:, :]), "sg", 16)
            MK2["tddmk"] = M("sg")
        PL(pl_tdd)

        def pe_wcs(t):
            W(t, "sg")
            W(t, "sa")
            wb(t, 3)
            for j in range(NCH):
                t.matmul(pb[3][:, 0:1], a_bf[:, 128 * j:128 * (j + 1)], sb["onesb"][:, :],
                         start=(j == 0), stop=(j == NCH - 1))
            ii = None
            for j in range(NCH):
                ii = t.matmul(pb[3][0:8, 2:3], wr_bf[:, 8 * j:8 * (j + 1)], sb["onesb"][:, :],
                              start=(j == 0), stop=(j == NCH - 1))
            I(ii, "sp")
        PE(pe_wcs)

        def dve_wcs(v):
            W(v, "sp")
            v.tensor_copy(sb["wk"][:, 0:1], pb[3][:, 0:1])
            I(v.tensor_copy(sb["wk"][0:8, 1:2], pb[3][0:8, 2:3]), "sv")
        DVE(dve_wcs)
        bank_last[3] = M("sv")

        def pe_wcs2(t):
            W(t, "sv")
            wb(t, 4)
            t.transpose(pb[4][0:1, 0:128], sb["wk"][:, 0:1], sb["idn"][:, :])
            I(t.transpose(pb[4][0:1, 128:136], sb["wk"][0:8, 1:2], sb["idn"][0:8, 0:8]), "sp")
        PE(pe_wcs2)

        def dve_wcs2(v):
            W(v, "sp")
            v.tensor_scalar(sb["anegrow"][:, :], pb[4][0:1, 0:128], -1.0, None, ALU.mult)
            I(v.tensor_scalar(sb["wrnegrow"][:, :], pb[4][0:1, 128:136], -1.0, None, ALU.mult), "sv")
        DVE(dve_wcs2)
        bank_last[4] = M("sv")
        MK2["convready"] = M("sv")

        for k in range(NT // 2):
            tA, tB = 2 * k, 2 * k + 1

            if k >= 1:
                for g in range(NG):
                    emit_build(tA, g)
                emit_stats(tA, xT[0])
                emit_ln("ln1", 0)
                for g in range(NG):
                    emit_build(tB, g)
                emit_stats(tB, xT[1])
                emit_ln("ln1", 1)

            emit_lnT("ln1", 0)
            emit_lnT("ln1", 1)

            NC2 = NCH // 2
            epimark = {}
            epi_pend = []
            epi_done = []
            for r2 in range(NC2):
                q, subq = divmod(r2, 2)
                ckmms = {}
                for par, t_i in ((0, tA), (1, tB)):
                    obk = (0, 1) if par == 0 else (2, 3)
                    sbank = 4 + par
                    zfs, zbs = sb["zf" + str(par)], sb["zb" + str(par)]

                    def pe_ck(t, t_i=t_i, par=par, r2=r2, q=q, subq=subq, obk=obk,
                              sbank=sbank, zfs=zfs, zbs=zbs):
                        if r2 == 0 and t_i == 0:
                            WV(t, "sv", MK2["convready"])
                            WV(t, "sg", MK2["tddmk"])
                        if subq == 0:
                            wb(t, *obk)
                            WV(t, "sv", MK[("negmb", par)])
                            for g in range(NG):
                                t.matmul(pb[obk[g]][:, 0:512],
                                         sb["negmb" + str(par)][0:1, 128 * g:128 * g + 128],
                                         sb["tdd"][0:1, 512 * q:512 * q + 512], start=True, stop=False)
                        ii = None
                        c0 = RT * (2 * r2)
                        c1 = RT * (2 * r2 + 1)
                        if r2 > 0:
                            WV(t, "sv", MK[("zev", t_i, r2 - 1)])
                            for g in range(NG):
                                ii = t.matmul(pb[obk[g]][:, 256 * subq:256 * subq + 256],
                                              zbs[:, 128 * g:128 * g + 128], sb["e_bfm"][:, :],
                                              start=False, stop=False)
                        for g in range(NG):
                            t.matmul(pb[obk[g]][:, 256 * subq:256 * subq + 256],
                                     xT[par][:, c0 + 128 * g:c0 + 128 * g + 128],
                                     sb["t01_bfm"][:, :], start=False, stop=False)
                            ii = t.matmul(pb[obk[g]][:, 256 * subq + 128:256 * subq + 256],
                                          xT[par][:, c1 + 128 * g:c1 + 128 * g + 128],
                                          sb["t01_bfm"][:, 0:128], start=False,
                                          stop=(subq == 1))
                        if r2 < NC2 - 1:
                            wb(t, sbank)
                            if r2 == 0:
                                t.matmul(pb[sbank][:, 0:RT], sb["p0_bfm"][:, :],
                                         xT[par][:, c0:c0 + RT], start=True, stop=False)
                                ii = t.matmul(pb[sbank][:, 0:RT], sb["p1_bfm"][:, :],
                                              xT[par][:, c1:c1 + RT], start=False, stop=True)
                            else:
                                t.matmul(pb[sbank][:, 0:RT], sb["p0_bfm"][:, :],
                                         xT[par][:, c0:c0 + RT], start=True, stop=False)
                                t.matmul(pb[sbank][:, 0:RT], sb["p1_bfm"][:, :],
                                         xT[par][:, c1:c1 + RT], start=False, stop=False)
                                ii = t.matmul(pb[sbank][:, 0:RT], sb["a128t"][:, :], zfs[:, :],
                                              start=False, stop=True)
                        I(ii, "sp")
                    PE(pe_ck)
                    ckmms[par] = M("sp")

                    if r2 < NC2 - 1:
                        def dve_z(v, ckmm=ckmms[par], t_i=t_i, r2=r2, sbank=sbank, zfs=zfs, zbs=zbs):
                            v.wait_ge(sems["sp"][0], ckmm)
                            v.tensor_copy(zfs[:, :], pb[sbank][:, 0:RT])
                            I(v.tensor_copy(zbs[:, :], pb[sbank][:, 0:RT]), "sv")
                        DVE(dve_z)
                        MK[("zev", t_i, r2)] = M("sv")
                        bank_last[sbank] = M("sv")

                if r2 == 0:
                    def act_prem(a):
                        WV(a, "sv", MK[("lnC", 0)])
                        WV(a, "sv", MK[("lnC", 1)])
                        for qq in range(8):
                            for par in range(2):
                                for g in range(NG):
                                    ps = x[par][:, DM * g + 512 * qq:DM * g + 512 * qq + 512]
                                    I(a.activation(ps, ps, AF.Copy, scale=sb["sc1c" + str(par)][:, g:g + 1]), "sa")
                                    MK[("prem", par, g, qq)] = M("sa")
                    ACT(act_prem)

                if subq == 1:
                    for par, t_i in ((0, tA), (1, tB)):
                        obk = (0, 1) if par == 0 else (2, 3)
                        epi_pend.append((par, t_i, q, obk, ckmms[par]))
                    epi_done = epi_pend
                    epi_pend = []
                    for par, t_i, q_, obk, ckmm in epi_done:
                        def dve_epi(v, t_i=t_i, q=q_, obk=obk, ckmm=ckmm, par=par):
                            v.wait_ge(sems["sp"][0], ckmm)
                            for g in range(NG):
                                WV(v, "sa", MK[("prem", par, g, q)])
                                rng = slice(DM * g + 512 * q, DM * g + 512 * q + 512)
                                I(v.scalar_tensor_tensor(x[par][:, rng], pb[obk[g]][:, 0:512],
                                                         sb["invsc" + str(par)][:, g:g + 1], x[par][:, rng],
                                                         ALU.mult, ALU.add), "sv")
                        DVE(dve_epi)
                        for b_ in obk:
                            bank_last[b_] = M("sv")
                        epimark[t_i] = M("sv")
                    epi_done = []

            for par, t_i, q_, obk, ckmm in epi_done + epi_pend:
                def dve_epi(v, t_i=t_i, q=q_, obk=obk, ckmm=ckmm, par=par):
                    v.wait_ge(sems["sp"][0], ckmm)
                    for g in range(NG):
                        WV(v, "sa", MK[("prem", par, g, q)])
                        rng = slice(DM * g + 512 * q, DM * g + 512 * q + 512)
                        I(v.scalar_tensor_tensor(x[par][:, rng], pb[obk[g]][:, 0:512],
                                                 sb["invsc" + str(par)][:, g:g + 1], x[par][:, rng],
                                                 ALU.mult, ALU.add), "sv")
                DVE(dve_epi)
                for b_ in obk:
                    bank_last[b_] = M("sv")
                epimark[t_i] = M("sv")

            for par, t_i in ((0, tA), (1, tB)):
                s_ = lambda nm, par=par: sb[nm + str(par)]
                for g in range(NG):
                    MK[("epi", t_i, g)] = epimark[t_i]
                def act_cast2(a, t_i=t_i, par=par):
                    for g in range(NG):
                        slot = 2 * par + (g % 2)
                        WV(a, "sv", MK[("epi", t_i, g)])
                        WV(a, "sp", MK.get(("slotfree", slot), 0))
                        I(a.activation(xb[slot][:, :], x[par][:, DM * g:DM * (g + 1)], AF.Copy), "sa")
                        MK[("cast2", t_i, g)] = M("sa")
                ACT(act_cast2)
                for g in range(NG):
                    slot = 2 * par + (g % 2)
                    emit_tr(t_i, g, xTn, ("cast2", t_i, g), slot)
                emit_stats(t_i, xTn)
                emit_ln("ln2", par)

                def pe_router(t, par=par):
                    wb(t, 1)
                    ii = None
                    for j in range(NCH):
                        ii = t.matmul(pb[1][0:8, 0:RT], wr_bf[:, 8 * j:8 * (j + 1)],
                                      xTn[:, RT * j:RT * j + RT], start=(j == 0), stop=False)
                    WV(t, "sv", MK[("negmb", par)])
                    I(t.matmul(pb[1][0:8, 0:RT], sb["wrnegrow"][:, :],
                               sb["posmb" + str(par)][0:1, 0:RT], start=False, stop=True), "sp")
                PE(pe_router)

                def dve_raw8(v):
                    W(v, "sp")
                    I(v.tensor_copy(sb["raw8"][:, :], pb[1][0:8, 0:RT]), "sv")
                DVE(dve_raw8)
                bank_last[1] = M("sv")
                MK[("raw8",)] = M("sv")

                def pe_lt(t):
                    WV(t, "sv", MK[("raw8",)])
                    wb(t, 5)
                    ii = None
                    for g in range(NG):
                        ii = t.transpose(pb[5][:, 8 * g:8 * g + 8], sb["raw8"][0:8, 128 * g:128 * (g + 1)], sb["idn"][0:8, 0:8])
                    I(ii, "sp")
                PE(pe_lt)
                ltmm = M("sp")

                emit_lnT("ln2", par)

                def pe_low(t, par=par):
                    wb(t, 0)
                    for j in range(NCH):
                        t.matmul(pb[0][:, 0:RT], a_bf[:, 128 * j:128 * (j + 1)],
                                 xTn[:, RT * j:RT * j + RT], start=(j == 0), stop=False)
                    I(t.matmul(pb[0][:, 0:RT], sb["anegrow"][:, :],
                               sb["posmb" + str(par)][0:1, 0:RT], start=False, stop=True), "sp")
                PE(pe_low)

                def dve_sm1(v, ltmm=ltmm, par=par):
                    v.wait_ge(sems["sp"][0], ltmm)
                    WV(v, "sv", MK[("lnC", par)])
                    v.tensor_copy(sb["scrt"][:, :], pb[5][:, 0:8 * NG])
                    v.drain()
                    for g in range(NG):
                        v.tensor_scalar(sb["rawt"][:, 8 * g:8 * (g + 1)], sb["scrt"][:, 8 * g:8 * (g + 1)],
                                        sb["invsc" + str(par)][:, g:g + 1], None, ALU.mult)
                    v.drain()
                    for g in range(NG):
                        v.tensor_add(sb["scrt"][:, 8 * g:8 * (g + 1)], sb["rawt"][:, 8 * g:8 * (g + 1)], sb["brrep"][:, :])
                    v.drain()
                    for g in range(NG):
                        v.reduce_max(sb["smx"][:, g:g + 1], sb["scrt"][:, 8 * g:8 * (g + 1)], axis=AX.X)
                    v.drain()
                    ii = None
                    for g in range(NG):
                        ii = v.tensor_scalar(sb["rawt"][:, 8 * g:8 * (g + 1)], sb["scrt"][:, 8 * g:8 * (g + 1)],
                                             sb["smx"][:, g:g + 1], None, ALU.subtract)
                    I(ii, "sv")
                DVE(dve_sm1)
                bank_last[5] = M("sv")

                def act_sm(a):
                    W(a, "sv")
                    I(a.activation(sb["scrt"][:, :], sb["rawt"][:, :], AF.Exp), "sa")
                ACT(act_sm)

                def dve_sm2(v):
                    W(v, "sa")
                    for g in range(NG):
                        v.reduce_sum(sb["ssum"][:, g:g + 1], sb["scrt"][:, 8 * g:8 * (g + 1)], axis=AX.X)
                    v.drain()
                    v.reciprocal(sb["srec"][:, :], sb["ssum"][:, :])
                    v.drain()
                    for g in range(NG):
                        v.tensor_scalar(sb["rawt"][:, 8 * g:8 * (g + 1)], sb["scrt"][:, 8 * g:8 * (g + 1)],
                                        sb["srec"][:, g:g + 1], 8.0, ALU.mult, ALU.mult)
                    v.drain()
                    I(v.tensor_copy(sb["scrt"][:, :], sb["rawt"][:, :]), "sv")
                DVE(dve_sm2)
                MK[("scores",)] = M("sv")

                def pe_st2(t):
                    WV(t, "sv", MK[("scores",)])
                    wb(t, 3)
                    ii = None
                    for g in range(NG):
                        ii = t.transpose(pb[3][0:8, 128 * g:128 * (g + 1)], sb["scrt"][:, 8 * g:8 * (g + 1)], sb["idn"][:, :])
                    I(ii, "sp")
                PE(pe_st2)

                def dve_scfm(v):
                    W(v, "sp")
                    I(v.tensor_copy(sb["scfm"][:, :], pb[3][0:8, 0:RT]), "sv")
                DVE(dve_scfm)
                bank_last[3] = M("sv")
                MK[("scfm",)] = M("sv")

                def pe_emm(t):
                    WV(t, "sv", MK[("scfm",)])
                    wb(t, 4)
                    I(t.matmul(pb[4][:, 0:RT], sb["e8"][:, :], sb["scfm"][:, :], start=True, stop=True), "sp")
                PE(pe_emm)

                def dve_lows(v):
                    W(v, "sp")
                    v.tensor_copy(sb["se"][:, :], pb[4][:, 0:RT])
                    v.drain()
                    I(v.tensor_mul(sb["lows"][:, :], pb[0][:, 0:RT], sb["se"][:, :]), "sv")
                DVE(dve_lows)
                bank_last[4] = M("sv")
                bank_last[0] = M("sv")
                MK[("lows",)] = M("sv")

                for g in range(NG):
                    for o in range(8):
                        def pe_op(t, g=g, o=o):
                            if o == 0 and g == 0:
                                WV(t, "sv", MK[("lows",)])
                            wb(t, 2 + (o % 2))
                            bk = pb[2 + (o % 2)]
                            I(t.matmul(bk[:, 0:512], sb["lows"][:, 128 * g:128 * (g + 1)],
                                       b_bf[:, 512 * o:512 * o + 512], start=True, stop=True), "sp")
                        PE(pe_op)
                        opmm = M("sp")

                        def dve_fin(v, g=g, o=o, opmm=opmm, par=par):
                            v.wait_ge(sems["sp"][0], opmm)
                            bk = pb[2 + (o % 2)]
                            rng = slice(DM * g + 512 * o, DM * g + 512 * o + 512)
                            I(v.scalar_tensor_tensor(x[par][:, rng], bk[:, 0:512],
                                                     sb["sc2c" + str(par)][:, g:g + 1],
                                                     x[par][:, rng], ALU.mult, ALU.add), "sv")
                        DVE(dve_fin)
                        bank_last[2 + (o % 2)] = M("sv")
                    finmark = M("sv")

                    def sp_store(sp, g=g, t_i=t_i, par=par, finmark=finmark):
                        WV(sp, "sv", finmark)
                        rows = t_i * RT + g * 128
                        I(sp.dma_start(out=y[rows:rows + 128, :], in_=x[par][:, DM * g:DM * (g + 1)]), "st", 16)
                    SP(sp_store)
                st_marks[t_i] = M("st")
                if t_i + 2 < NT:
                    emit_load(t_i + 2)

        def sp_fin(sp):
            W(sp, "st")
        SP(sp_fin)

    return nc


_NC = None


def _get_nc():
    global _NC
    if _NC is None:
        _NC = build_nc()
    return _NC


def kernel(L0_attn_u, L0_attn_Lambda_ri, L0_attn_p_ri, L0_attn_q_ri, L0_attn_Vc_ri,
           L0_attn_Ct_ri, L0_attn_Bp, L0_attn_D, L0_attn_log_step,
           L0_ffn_A, L0_ffn_B, L0_ffn_Wr, L0_ffn_br):
    f32 = np.float32
    u = np.ascontiguousarray(np.asarray(L0_attn_u, dtype=f32).reshape(NROWS, DM))
    lam = np.asarray(L0_attn_Lambda_ri, dtype=f32)
    p_ri = np.asarray(L0_attn_p_ri, dtype=f32)
    q_ri = np.asarray(L0_attn_q_ri, dtype=f32)
    vc = np.asarray(L0_attn_Vc_ri, dtype=f32)
    ct = np.asarray(L0_attn_Ct_ri, dtype=f32).reshape(64, 2)
    bp = np.asarray(L0_attn_Bp, dtype=f32).reshape(64, 1)
    D = np.asarray(L0_attn_D, dtype=f32).reshape(1, 1)
    lst = np.asarray(L0_attn_log_step, dtype=f32).reshape(1, 1)
    A = np.asarray(L0_ffn_A, dtype=f32)
    B = np.asarray(L0_ffn_B, dtype=f32)
    Wr = np.asarray(L0_ffn_Wr, dtype=f32)
    br = np.asarray(L0_ffn_br, dtype=f32).reshape(1, 8)

    h_A = np.ascontiguousarray(
        A.transpose(2, 0, 1).reshape(DM, 128).reshape(NCH, 128, 128).transpose(1, 0, 2).reshape(128, DM))
    h_B = np.ascontiguousarray(B.transpose(0, 2, 1).reshape(128, DM))
    h_Wr = np.ascontiguousarray(
        Wr.T.reshape(NCH, 128, 8).transpose(1, 0, 2).reshape(128, NCH * 8))

    p_vec = np.ascontiguousarray(np.concatenate([
        lam[:, 0:1], lam[:, 1:2], p_ri[:, 0:1], p_ri[:, 1:2],
        q_ri[:, 0:1], q_ri[:, 1:2], ct[:, 0:1], ct[:, 1:2], bp,
        vc[:, :, 0].T, vc[:, :, 1].T], axis=1))
    p_row = np.ascontiguousarray(np.concatenate([D, lst, br], axis=1))

    cst = _consts()
    base_map = dict(cst)
    base_map.update({
        "w_A": h_A, "w_B": h_B, "w_Wr": h_Wr,
        "p_vec": p_vec, "p_row": p_row,
    })

    in_maps = []
    for c in range(NCORES):
        m = dict(base_map)
        m["u"] = u[c * RPC:(c + 1) * RPC]
        in_maps.append(m)

    nc = _get_nc()
    res = run_bass_kernel_spmd(nc, in_maps, core_ids=list(range(NCORES)))
    out = np.concatenate([np.asarray(res.results[c]["y"]) for c in range(NCORES)], axis=0)
    return out.reshape(2, L, DM).astype(np.float32)


# version 30
# speedup vs baseline: 1.0058x; 1.0058x over previous
import numpy as np
import ml_dtypes
import contextlib
import concourse.bass as bass
import concourse.mybir as mybir
from concourse.bass_utils import run_bass_kernel_spmd

F32 = mybir.dt.float32
BF16 = mybir.dt.bfloat16
AF = mybir.ActivationFunctionType
ALU = mybir.AluOpType
AX = mybir.AxisListType

L = 4096
DM = 4096
NROWS = 8192
NCORES = 8
RPC = NROWS // NCORES
RT = 256
NT = RPC // RT
NG = RT // 128
NCH = DM // 128
EPS = 1e-5


def _host_consts():
    f32 = np.float32
    bf16 = ml_dtypes.bfloat16
    cf = np.zeros((128, 529), dtype=f32)
    cf[:, 0:128] = np.eye(128, dtype=f32)
    cf[:, 128:256] = np.eye(128)[::-1]
    sg = np.ones((128, 128), dtype=f32)
    sg[0:64, 64:128] = -1.0
    sg[64:128, 0:64] = -1.0
    cf[:, 256:384] = sg
    cf[:, 384:385] = 1.0
    cf[0, 385:513] = 1.0
    k_, r_ = np.meshgrid(np.arange(16), np.arange(16), indexing="ij")
    cf[0:16, 513:529] = (k_ < r_).astype(f32)
    cb = np.zeros((128, 385), dtype=np.float32)
    cb[:, 0:128] = np.eye(128)
    cb[:, 128:256] = np.eye(128)[::-1]
    cb[:, 256:257] = 1.0
    for h in range(8):
        cb[h, 257 + 16 * h:257 + 16 * (h + 1)] = 1.0
    return {"c_f32a": cf, "c_bfa": cb.astype(bf16)}


_CONSTS = None


def _consts():
    global _CONSTS
    if _CONSTS is None:
        _CONSTS = _host_consts()
    return _CONSTS


def build_nc():
    nc = bass.Bass("TRN2", target_bir_lowering=False, debug=False)

    u = nc.dram_tensor("u", [RPC, DM], F32, kind="ExternalInput").ap()
    y = nc.dram_tensor("y", [RPC, DM], F32, kind="ExternalOutput").ap()
    kscr = nc.dram_tensor("kscr", [512], BF16, kind="Internal")

    din = {}
    for nm, shp, dt in [
        ("c_f32a", [128, 529], F32), ("c_bfa", [128, 385], BF16),
        ("w_A", [128, DM], F32), ("w_B", [128, DM], F32), ("w_Wr", [128, 256], F32),
        ("p_vec", [64, 137], F32), ("p_row", [1, 10], F32),
    ]:
        din[nm] = nc.dram_tensor(nm, shp, dt, kind="ExternalInput").ap()

    ex = contextlib.ExitStack()
    T = lambda name, shape, dt: ex.enter_context(nc.sbuf_tensor(name, shape, dt))

    with ex:
        x = [T("x0", [128, NG * DM], F32), T("x1", [128, NG * DM], F32)]
        xb = [T(f"xb{i}", [128, DM], BF16) for i in range(4)]
        xT = [T("xT0", [128, NCH * RT], BF16), T("xT1", [128, NCH * RT], BF16)]
        xTn = T("xTn", [128, NCH * RT], BF16)
        sqT = T("sqT", [128, NCH * RT // 2], BF16)
        a_bf = T("a_bf", [128, DM], BF16)
        b_bf = T("b_bf", [128, DM], BF16)
        wr_bf = T("wr_bf", [128, 256], BF16)
        cf32 = T("cf32", [128, 529], F32)
        cbfa = T("cbfa", [128, 385], BF16)
        pvec = T("pvec", [64, 137], F32)
        prow = T("prow", [1, 10], F32)
        sb = {
            "idn": cf32[:, 0:128], "flip32": cf32[:, 128:256], "sgn": cf32[:, 256:384],
            "ones128c": cf32[:, 384:385], "ones1x128": cf32[0:1, 385:513],
            "ones64": cf32[0:1, 385:449], "ut16": cf32[0:16, 513:529],
            "idnb": cbfa[:, 0:128], "flip": cbfa[:, 128:256],
            "onesb": cbfa[:, 256:257], "e8": cbfa[0:8, 257:385],
            "lamre": pvec[:, 0:1], "lamim": pvec[:, 1:2],
            "pre": pvec[:, 2:3], "pim": pvec[:, 3:4],
            "qre": pvec[:, 4:5], "qim": pvec[:, 5:6],
            "ctre": pvec[:, 6:7], "ctim": pvec[:, 7:8], "bp": pvec[:, 8:9],
            "vctre": pvec[:, 9:73], "vctim": pvec[:, 73:137],
            "dsc": prow[0:1, 0:1], "lst": prow[0:1, 1:2], "brv": prow[0:1, 2:10],
        }
        for nm, shp, dt in [
            ("stp", [1, 1], F32), ("sh", [1, 1], F32),
            ("sh64", [64, 1], F32), ("stp64", [64, 1], F32),
            ("bre", [64, 1], F32), ("bim", [64, 1], F32),
            ("wood", [64, 48], F32), ("cells", [1, 24], F32), ("rows6", [1, 512], F32),
            ("c2col", [128, 1], F32), ("c3col", [128, 1], F32),
            ("bprcol", [128, 1], F32), ("bprjcol", [128, 1], F32),
            ("kcol2", [128, 2], F32), ("t01cscol", [128, 2], F32),
            ("t01csrow", [1, 256], F32), ("drow", [1, 256], F32),
            ("klineb", [1, 512], BF16), ("hk", [128, 256], BF16),
            ("a128t", [128, 128], F32),
            ("p0_bfm", [128, 128], BF16), ("p1_bfm", [128, 128], BF16),
            ("e_bfm", [128, 256], BF16),
            ("t01_bfm", [128, 256], BF16), ("ronb", [128, 16], BF16),
            ("zf0", [128, RT], F32), ("zb0", [128, RT], BF16),
            ("zf1", [128, RT], F32), ("zb1", [128, RT], BF16),
            ("tddgt", [16, 256], BF16), ("tdd", [1, DM], BF16),
            ("anegrow", [1, 128], BF16), ("wrnegrow", [1, 8], BF16),
            ("brrep", [128, 8], F32), ("wk", [128, 2], F32),
            ("sumr", [1, RT], F32), ("sum2r", [1, RT], F32),
            ("wkr", [1, RT], F32), ("wkr2", [1, RT], F32),
            ("wkr30", [1, RT], F32), ("invsr0", [1, RT], F32), ("sc1r0", [1, RT], F32),
            ("negmb0", [1, RT], BF16), ("posmb0", [1, RT], BF16),
            ("invsc0", [128, NG], F32), ("sc1c0", [128, NG], F32), ("sc2c0", [128, NG], F32),
            ("wkr31", [1, RT], F32), ("invsr1", [1, RT], F32), ("sc1r1", [1, RT], F32),
            ("negmb1", [1, RT], BF16), ("posmb1", [1, RT], BF16),
            ("invsc1", [128, NG], F32), ("sc1c1", [128, NG], F32), ("sc2c1", [128, NG], F32),
            ("raw8", [8, RT], F32), ("rawt", [128, 8 * NG], F32),
            ("scrt", [128, 8 * NG], F32), ("smx", [128, NG], F32),
            ("ssum", [128, NG], F32), ("srec", [128, NG], F32),
            ("scfm", [8, RT], BF16), ("se", [128, RT], BF16),
            ("lows", [128, RT], BF16),
        ]:
            sb[nm] = T(nm, shp, dt)

        xTnf = xTn[:, :].bitcast(F32)

        def SLOT(k, w=128):
            return xTnf[:, 128 * k:128 * k + w]

        S_MROT = [0, 2]
        S_TROT = [1, 3]
        TKEEP = {1: 10, 2: 4, 4: 5, 8: 6, 16: 7, 32: 8, 64: 9,
                 128: 18, 512: 19, 1024: 20, 2048: 21}
        SGNM = {1: 11, 2: 12, 4: 13, 8: 14, 16: 15, 32: 16, 64: 17, 128: 22}
        S_AR, S_V, S_U, S_CM, S_CMT, S_ADG, S_VTH = 22, 23, 25, 27, 28, 29, 31

        astage = xT[0][:, :].bitcast(F32)[:, 0:4096]
        bstage = xT[1][:, :].bitcast(F32)[:, 0:4096]
        wrstage = xTnf[:, 128 * 30:128 * 30 + 256]

        pb = [ex.enter_context(nc.psum_tensor(f"pb{i}", [128, 512], F32)) for i in range(6)]

        sems = {}
        for nm in ["sg", "sp", "sa", "sv", "st", "sw0", "sw1", "sw2", "sk",
                   "si00", "si01", "si10", "si11"]:
            sems[nm] = [ex.enter_context(nc.semaphore(nm)), 0]

        def W(eng, s):
            eng.wait_ge(sems[s][0], sems[s][1])

        def WV(eng, s, val):
            if val > 0:
                eng.wait_ge(sems[s][0], val)

        def I(instr, s, n=1):
            instr.then_inc(sems[s][0], n)
            sems[s][1] += n
            return instr

        def M(s):
            return sems[s][1]

        block = ex.enter_context(nc.Block())
        SP, PL, PE, ACT, DVE = block.sync, block.gpsimd, block.tensor, block.scalar, block.vector
        MK2 = {}
        MK = {}
        bank_last = {b: 0 for b in range(6)}
        bank_last_a = {b: 0 for b in range(6)}

        def wb(t, *banks):
            for b in banks:
                WV(t, "sv", bank_last[b])
                WV(t, "sa", bank_last_a[b])

        def emit_tr(t_i, g, dst, cast_key, slot):
            for c in range(8):
                def pe_tr(t, g=g, c=c, slot=slot):
                    if c == 0:
                        WV(t, "sa", MK[cast_key])
                    wb(t, c % 2)
                    bk = pb[c % 2]
                    ii = None
                    for jj in range(4):
                        j = 4 * c + jj
                        ii = t.matmul(bk[:, 128 * jj:128 * (jj + 1)],
                                      xb[slot][:, 128 * j:128 * (j + 1)], sb["idnb"][:, :],
                                      start=True, stop=True)
                    I(ii, "sp")
                PE(pe_tr)
                mm = M("sp")

                def ev(e, g=g, c=c, mm=mm, t_i=t_i):
                    e.wait_ge(sems["sp"][0], mm)
                    if t_i < 2 and g == 0 and c == 0:
                        WV(e, "sa", MK2["castA" if t_i % 2 == 0 else "castB"])
                    bk = pb[c % 2]
                    dst_ap = dst[:, :].rearrange("p (j r) -> p j r", j=NCH)[:, 4 * c:4 * c + 4, 128 * g:128 * (g + 1)]
                    src_ap = bk[:, 0:512].rearrange("p (j r) -> p j r", j=4)
                    I(e.tensor_copy(dst_ap, src_ap), "sv")
                DVE(ev)
                bank_last[c % 2] = M("sv")
            MK[("slotfree", slot)] = M("sp")

        QW = NCH * RT // 4

        def emit_stats(t_i, src):
            def pe_sx(t):
                wb(t, 4, 5)
                ii = None
                for j in range(NCH):
                    ii = t.matmul(pb[4][0:1, 0:RT], sb["onesb"][:, :], src[:, RT * j:RT * j + RT],
                                  start=(j == 0), stop=(j == NCH - 1))
                I(ii, "sp")
            PE(pe_sx)
            qmarks = {}
            for q in range(4):
                def dve_sq(v, q=q):
                    if q >= 2:
                        WV(v, "sp", qmarks[q - 2])
                    h0 = (q % 2) * QW
                    I(v.tensor_mul(sqT[:, h0:h0 + QW], src[:, q * QW:(q + 1) * QW],
                                   src[:, q * QW:(q + 1) * QW]), "sv")
                DVE(dve_sq)
                MK[("sqq", q)] = M("sv")

                def pe_q(t, q=q):
                    WV(t, "sv", MK[("sqq", q)])
                    ii = None
                    for j in range(8):
                        ii = t.matmul(pb[5][0:1, 0:RT], sb["onesb"][:, :],
                                      sqT[:, (q % 2) * QW + RT * j:(q % 2) * QW + RT * j + RT],
                                      start=(q == 0 and j == 0), stop=(q == 3 and j == 7))
                    I(ii, "sp")
                PE(pe_q)
                qmarks[q] = M("sp")

        def emit_ln(kind, par):
            s_ = lambda nm: sb[nm + str(par)]

            def dve_lnA(v):
                W(v, "sp")
                v.tensor_copy(sb["sumr"][:, :], pb[4][0:1, 0:RT])
                v.tensor_copy(sb["sum2r"][:, :], pb[5][0:1, 0:RT])
                v.tensor_scalar(s_("negmb")[:, :], sb["sumr"][:, :], -1.0 / DM, None, ALU.mult)
                if kind == "ln2":
                    v.tensor_scalar(s_("posmb")[:, :], sb["sumr"][:, :], 1.0 / DM, None, ALU.mult)
                v.tensor_scalar(sb["wkr"][:, :], sb["sumr"][:, :], 1.0 / DM, None, ALU.mult)
                v.tensor_scalar(s_("wkr3")[:, :], sb["sum2r"][:, :], 1.0 / DM, None, ALU.mult)
                v.tensor_mul(sb["wkr2"][:, :], sb["wkr"][:, :], sb["wkr"][:, :])
                v.tensor_sub(s_("wkr3")[:, :], s_("wkr3")[:, :], sb["wkr2"][:, :])
                v.tensor_scalar(s_("wkr3")[:, :], s_("wkr3")[:, :], EPS, None, ALU.add)
                v.drain()
                I(v.reciprocal(s_("wkr3")[:, :], s_("wkr3")[:, :]), "sv")
            DVE(dve_lnA)
            bank_last[4] = M("sv")
            bank_last[5] = M("sv")
            MK[("negmb", par)] = M("sv")

            def act_ln(a):
                W(a, "sv")
                I(a.activation(s_("invsr")[:, :], s_("wkr3")[:, :], AF.Sqrt), "sa")
            ACT(act_ln)

            def dve_lnB(v):
                W(v, "sa")
                if kind == "ln1":
                    I(v.tensor_scalar(s_("sc1r")[:, :], s_("invsr")[:, :], sb["dsc"][:, :], 1.0, ALU.mult, ALU.add), "sv")
                else:
                    I(v.tensor_scalar(s_("sc1r")[:, :], s_("invsr")[:, :], 2.0, None, ALU.mult), "sv")
            DVE(dve_lnB)
            MK[("lnB", par)] = M("sv")

        def emit_lnT(kind, par):
            s_ = lambda nm: sb[nm + str(par)]

            def pe_ln(t):
                WV(t, "sv", MK[("lnB", par)])
                wb(t, 4)
                ii = None
                for g in range(NG):
                    t.matmul(pb[4][:, g:g + 1], s_("invsr")[0:1, 128 * g:128 * (g + 1)],
                             sb["ones1x128"][0:1, 0:1], start=True, stop=True)
                for g in range(NG):
                    ii = t.matmul(pb[4][:, NG + g:NG + g + 1], s_("sc1r")[0:1, 128 * g:128 * (g + 1)],
                                  sb["ones1x128"][0:1, 0:1], start=True, stop=True)
                I(ii, "sp")
            PE(pe_ln)
            lnmm = M("sp")

            def dve_lnC(v, lnmm=lnmm):
                v.wait_ge(sems["sp"][0], lnmm)
                v.tensor_copy(s_("invsc")[:, :], pb[4][:, 0:NG])
                I(v.tensor_copy(s_("sc1c" if kind == "ln1" else "sc2c")[:, :], pb[4][:, NG:2 * NG]), "sv")
            DVE(dve_lnC)
            bank_last[4] = M("sv")
            MK[("lnC", par)] = M("sv")

        def emit_build(t_i, g):
            p = t_i % 2
            slot = 2 * p + (g % 2)
            def act_cast(a, t_i=t_i, g=g, p=p, slot=slot):
                WV(a, *MK[("ld", t_i, g)])
                WV(a, "sp", MK.get(("slotfree", slot), 0))
                I(a.activation(xb[slot][:, :], x[p][:, DM * g:DM * (g + 1)], AF.Copy), "sa")
                MK[("cast", t_i, g)] = M("sa")
            ACT(act_cast)
            emit_tr(t_i, g, xT[p], ("cast", t_i, g), slot)

        st_marks = {}

        def emit_load(t_j):
            def sp_load(sp, t_j=t_j):
                if t_j >= 2:
                    WV(sp, "st", st_marks[t_j - 2])
                pq = t_j % 2
                for g in range(NG):
                    rows = t_j * RT + g * 128
                    snm = f"si{pq}{g}"
                    I(sp.dma_start(out=x[pq][:, DM * g:DM * (g + 1)], in_=u[rows:rows + 128, :]), snm, 16)
                    MK[("ld", t_j, g)] = (snm, M(snm))
            SP(sp_load)


        def pl_consts(g):
            g.memset(sb["klineb"][0:1, 0:127], 0.0)
            I(g.dma_start(out=pvec[:, :], in_=din["p_vec"]), "sg", 16)
            I(g.dma_start(out=prow[:, :], in_=din["p_row"]), "sg", 16)
            I(g.dma_start(out=cf32[:, :], in_=din["c_f32a"]), "sg", 16)
            I(g.dma_start(out=cbfa[:, :], in_=din["c_bfa"]), "sg", 16)
            MK2["params"] = M("sg")
            I(g.dma_start(out=astage, in_=din["w_A"]), "sw0", 16)
            I(g.dma_start(out=bstage, in_=din["w_B"]), "sw1", 16)
            I(g.dma_start(out=wrstage, in_=din["w_Wr"]), "sw2", 16)
        PL(pl_consts)

        def act_setup(a):
            WV(a, "sg", MK2["params"])
            I(a.activation(sb["stp"][:, :], sb["lst"][:, :], AF.Exp), "sa")
            MK2["stp"] = M("sa")
            WV(a, "sw0", 16)
            I(a.activation(a_bf[:, :], astage, AF.Copy), "sa")
            MK2["castA"] = M("sa")
        ACT(act_setup)

        def dve_sh(v):
            v.wait_ge(sems["sa"][0], MK2["stp"])
            I(v.tensor_scalar(sb["sh"][:, :], sb["stp"][:, :], 0.5, None, ALU.mult), "sv")
        DVE(dve_sh)

        emit_load(0)
        emit_load(1)

        def pe_setup1(t):
            W(t, "sv")
            wb(t, 0, 1)
            t.matmul(pb[0][0:64, 0:1], sb["ones64"][:, :], sb["sh"][:, :], start=True, stop=True)
            t.matmul(pb[0][0:64, 1:2], sb["ones64"][:, :], sb["stp"][:, :], start=True, stop=True)
            t.matmul(pb[0][:, 2:10], sb["ones1x128"][:, :], sb["brv"][:, :], start=True, stop=True)
            t.matmul(pb[0][0:1, 16:144], sb["dsc"][:, :], sb["ones1x128"][:, :], start=True, stop=True)
            t.matmul(pb[0][0:1, 144:272], sb["dsc"][:, :], sb["ones1x128"][:, :], start=True, stop=True)
            t.matmul(pb[1][0:64, 0:1], sb["vctre"][:, :], sb["bp"][:, :], start=True, stop=True)
            I(t.matmul(pb[1][0:64, 1:2], sb["vctim"][:, :], sb["bp"][:, :], start=True, stop=True), "sp")
        PE(pe_setup1)

        wd = sb["wood"]
        WC = lambda c: wd[:, c:c + 1]

        def dve_wood1(v):
            D_ = v.drain
            W(v, "sp")
            D_()
            v.tensor_copy(sb["sh64"][:, :], pb[0][0:64, 0:1])
            D_()
            v.tensor_copy(sb["stp64"][:, :], pb[0][0:64, 1:2])
            D_()
            v.tensor_copy(sb["brrep"][:, :], pb[0][:, 2:10])
            D_()
            v.tensor_copy(sb["drow"][:, :], pb[0][0:1, 16:272])
            D_()
            v.tensor_copy(sb["bre"][:, :], pb[1][0:64, 0:1])
            D_()
            v.tensor_copy(sb["bim"][:, :], pb[1][0:64, 1:2])
            D_()
            sh64 = sb["sh64"][:, :]
            D_()
            v.tensor_scalar(WC(0), sb["lamre"][:, :], sh64, None, ALU.mult)
            D_()
            v.tensor_scalar(WC(1), sb["lamim"][:, :], sh64, None, ALU.mult)
            D_()
            v.tensor_scalar(WC(12), sb["pre"][:, :], sh64, None, ALU.mult)
            D_()
            v.tensor_scalar(WC(13), sb["pim"][:, :], sh64, None, ALU.mult)
            D_()
            v.tensor_scalar(WC(14), sb["qim"][:, :], -1.0, None, ALU.mult)
            D_()
            D_()
            v.tensor_scalar(WC(2), WC(0), -1.0, 1.0, ALU.mult, ALU.add)
            D_()
            v.tensor_scalar(WC(3), WC(1), -1.0, None, ALU.mult)
            D_()
            v.tensor_scalar(WC(4), WC(0), 1.0, None, ALU.add)
            D_()
            v.tensor_copy(WC(5), WC(1))
            D_()
            D_()
            v.tensor_mul(WC(6), WC(2), WC(2))
            D_()
            v.tensor_mul(WC(44), WC(3), WC(3))
            D_()
            D_()
            v.tensor_add(WC(6), WC(6), WC(44))
            D_()
            D_()
            v.reciprocal(WC(7), WC(6))
            D_()
            D_()
            v.tensor_mul(WC(8), WC(2), WC(7))
            D_()
            v.tensor_mul(WC(44), WC(3), WC(7))
            D_()
            D_()
            v.tensor_scalar(WC(9), WC(44), -1.0, None, ALU.mult)
            D_()
            v.tensor_mul(WC(44), WC(4), WC(2))
            D_()
            v.tensor_mul(WC(45), WC(5), WC(3))
            D_()
            D_()
            v.tensor_add(WC(10), WC(44), WC(45))
            D_()
            v.tensor_mul(WC(44), WC(5), WC(2))
            D_()
            v.tensor_mul(WC(45), WC(4), WC(3))
            D_()
            D_()
            v.tensor_sub(WC(11), WC(44), WC(45))
            D_()
            D_()
            v.tensor_mul(WC(10), WC(10), WC(7))
            D_()
            v.tensor_mul(WC(11), WC(11), WC(7))
            D_()
            v.tensor_mul(WC(44), WC(8), WC(12))
            D_()
            v.tensor_mul(WC(45), WC(9), WC(13))
            D_()
            D_()
            v.tensor_sub(WC(15), WC(44), WC(45))
            D_()
            v.tensor_mul(WC(44), WC(8), WC(13))
            D_()
            v.tensor_mul(WC(45), WC(9), WC(12))
            D_()
            D_()
            v.tensor_add(WC(16), WC(44), WC(45))
            D_()
            v.tensor_mul(WC(44), sb["qre"][:, :], WC(8))
            D_()
            v.tensor_mul(WC(45), WC(14), WC(9))
            D_()
            D_()
            v.tensor_sub(WC(17), WC(44), WC(45))
            D_()
            v.tensor_mul(WC(44), sb["qre"][:, :], WC(9))
            D_()
            v.tensor_mul(WC(45), WC(14), WC(8))
            D_()
            D_()
            v.tensor_add(WC(18), WC(44), WC(45))
            D_()
            D_()
            v.tensor_mul(WC(44), WC(17), WC(12))
            D_()
            v.tensor_mul(WC(45), WC(18), WC(13))
            D_()
            D_()
            v.tensor_sub(WC(19), WC(44), WC(45))
            D_()
            v.tensor_mul(WC(44), WC(17), WC(13))
            D_()
            v.tensor_mul(WC(45), WC(18), WC(12))
            D_()
            D_()
            v.tensor_add(WC(20), WC(44), WC(45))
            D_()
            v.tensor_mul(WC(44), WC(17), sb["bre"][:, :])
            D_()
            v.tensor_mul(WC(45), WC(18), sb["bim"][:, :])
            D_()
            D_()
            v.tensor_sub(WC(21), WC(44), WC(45))
            D_()
            v.tensor_mul(WC(44), WC(17), sb["bim"][:, :])
            D_()
            v.tensor_mul(WC(45), WC(18), sb["bre"][:, :])
            D_()
            D_()
            v.tensor_add(WC(22), WC(44), WC(45))
            D_()
            v.tensor_mul(WC(44), WC(17), WC(4))
            D_()
            v.tensor_mul(WC(45), WC(18), WC(5))
            D_()
            D_()
            v.tensor_sub(WC(23), WC(44), WC(45))
            D_()
            v.tensor_mul(WC(44), WC(17), WC(5))
            D_()
            v.tensor_mul(WC(45), WC(18), WC(4))
            D_()
            D_()
            I(v.tensor_add(WC(24), WC(44), WC(45)), "sv")
        DVE(dve_wood1)
        bank_last[0] = M("sv")
        bank_last[1] = M("sv")
        MK2["wd1"] = M("sv")

        def pe_reduce(t):
            WV(t, "sv", MK2["wd1"])
            I(t.matmul(pb[2][0:1, 0:4], sb["ones128c"][0:64, 0:1], wd[:, 19:23], start=True, stop=True), "sp")
        PE(pe_reduce)
        MK2["red"] = M("sp")

        ce = sb["cells"]
        CC = lambda c: ce[0:1, c:c + 1]

        def dve_wood2(v):
            D_ = v.drain
            v.wait_ge(sems["sp"][0], MK2["red"])
            D_()
            v.tensor_copy(ce[0:1, 0:4], pb[2][0:1, 0:4])
            D_()
            D_()
            v.tensor_scalar(CC(4), CC(0), 1.0, None, ALU.add)
            D_()
            D_()
            v.tensor_mul(CC(14), CC(4), CC(4))
            D_()
            v.tensor_mul(CC(15), CC(1), CC(1))
            D_()
            D_()
            v.tensor_add(CC(5), CC(14), CC(15))
            D_()
            D_()
            v.reciprocal(CC(16), CC(5))
            D_()
            D_()
            v.tensor_mul(CC(6), CC(4), CC(16))
            D_()
            v.tensor_mul(CC(14), CC(1), CC(16))
            D_()
            D_()
            v.tensor_scalar(CC(7), CC(14), -1.0, None, ALU.mult)
            D_()
            D_()
            v.tensor_mul(CC(14), CC(0), CC(6))
            D_()
            v.tensor_mul(CC(15), CC(1), CC(7))
            D_()
            D_()
            v.tensor_sub(CC(8), CC(14), CC(15))
            D_()
            D_()
            v.tensor_mul(CC(14), CC(0), CC(7))
            D_()
            v.tensor_mul(CC(15), CC(1), CC(6))
            D_()
            D_()
            v.tensor_add(CC(9), CC(14), CC(15))
            D_()
            v.tensor_scalar(CC(8), CC(8), 1.0, None, ALU.subtract)
            D_()
            v.tensor_scalar(CC(10), CC(6), -1.0, None, ALU.mult)
            D_()
            v.tensor_scalar(CC(11), CC(7), -1.0, None, ALU.mult)
            D_()
            v.tensor_mul(CC(14), CC(2), CC(6))
            D_()
            v.tensor_mul(CC(15), CC(3), CC(7))
            D_()
            D_()
            v.tensor_sub(CC(12), CC(14), CC(15))
            D_()
            D_()
            v.tensor_mul(CC(14), CC(2), CC(7))
            D_()
            v.tensor_mul(CC(15), CC(3), CC(6))
            D_()
            D_()
            I(v.tensor_add(CC(13), CC(14), CC(15)), "sv")
        DVE(dve_wood2)
        bank_last[2] = M("sv")
        MK2["wd2"] = M("sv")

        def pe_bcast(t):
            WV(t, "sv", MK2["wd2"])
            wb(t, 3)
            I(t.matmul(pb[3][0:64, 0:6], sb["ones64"][:, :], ce[0:1, 8:14], start=True, stop=True), "sp")
        PE(pe_bcast)
        MK2["bc"] = M("sp")

        def dve_wood3(v):
            D_ = v.drain
            v.wait_ge(sems["sp"][0], MK2["bc"])
            D_()
            v.tensor_copy(wd[:, 25:31], pb[3][0:64, 0:6])
            D_()
            D_()
            v.tensor_mul(WC(44), sb["qre"][:, :], WC(25))
            D_()
            v.tensor_mul(WC(45), WC(14), WC(26))
            D_()
            D_()
            v.tensor_sub(WC(31), WC(44), WC(45))
            D_()
            v.tensor_mul(WC(44), WC(23), WC(27))
            D_()
            v.tensor_mul(WC(45), WC(24), WC(28))
            D_()
            D_()
            v.tensor_sub(WC(40), WC(44), WC(45))
            D_()
            D_()
            v.tensor_add(WC(31), WC(31), WC(40))
            D_()
            v.tensor_mul(WC(44), sb["qre"][:, :], WC(26))
            D_()
            v.tensor_mul(WC(45), WC(14), WC(25))
            D_()
            D_()
            v.tensor_add(WC(32), WC(44), WC(45))
            D_()
            v.tensor_mul(WC(44), WC(23), WC(28))
            D_()
            v.tensor_mul(WC(45), WC(24), WC(27))
            D_()
            D_()
            v.tensor_add(WC(40), WC(44), WC(45))
            D_()
            D_()
            v.tensor_add(WC(32), WC(32), WC(40))
            D_()
            v.tensor_mul(WC(44), WC(8), sb["bre"][:, :])
            D_()
            v.tensor_mul(WC(45), WC(9), sb["bim"][:, :])
            D_()
            D_()
            v.tensor_sub(WC(33), WC(44), WC(45))
            D_()
            v.tensor_mul(WC(44), WC(8), sb["bim"][:, :])
            D_()
            v.tensor_mul(WC(45), WC(9), sb["bre"][:, :])
            D_()
            D_()
            v.tensor_add(WC(34), WC(44), WC(45))
            D_()
            v.tensor_mul(WC(44), WC(29), WC(15))
            D_()
            v.tensor_mul(WC(45), WC(30), WC(16))
            D_()
            D_()
            v.tensor_sub(WC(40), WC(44), WC(45))
            D_()
            D_()
            v.tensor_sub(WC(33), WC(33), WC(40))
            D_()
            v.tensor_mul(WC(44), WC(29), WC(16))
            D_()
            v.tensor_mul(WC(45), WC(30), WC(15))
            D_()
            D_()
            v.tensor_add(WC(40), WC(44), WC(45))
            D_()
            D_()
            v.tensor_sub(WC(34), WC(34), WC(40))
            D_()
            D_()
            v.tensor_scalar(WC(33), WC(33), sb["stp64"][:, :], None, ALU.mult)
            D_()
            v.tensor_scalar(WC(34), WC(34), sb["stp64"][:, :], None, ALU.mult)
            D_()
            v.tensor_scalar(WC(35), WC(15), -1.0, None, ALU.mult)
            D_()
            v.tensor_scalar(WC(36), WC(16), -1.0, None, ALU.mult)
            D_()
            v.tensor_scalar(WC(37), WC(11), -1.0, None, ALU.mult)
            D_()
            D_()
            v.tensor_copy(WC(38), WC(15))
            D_()
            v.tensor_copy(WC(39), WC(16))
            D_()
            v.tensor_copy(WC(40), WC(35))
            D_()
            v.tensor_copy(WC(41), WC(36))
            D_()
            v.tensor_copy(WC(42), WC(31))
            D_()
            v.tensor_copy(WC(43), WC(32))
            D_()
            D_()
            v.tensor_copy(sb["bprcol"][0:64, :], WC(33))
            D_()
            v.tensor_copy(sb["bprcol"][64:128, :], WC(34))
            D_()
            v.tensor_copy(sb["c2col"][0:64, :], sb["ctre"][:, :])
            D_()
            v.tensor_scalar(sb["c2col"][64:128, :], sb["ctim"][:, :], -1.0, None, ALU.mult)
            D_()
            v.tensor_copy(sb["c3col"][0:64, :], sb["ctre"][:, :])
            D_()
            v.tensor_copy(sb["c3col"][64:128, :], sb["ctim"][:, :])
            idq = sb["idn"][0:64, 0:64]
            adg = SLOT(S_ADG)
            D_()
            v.tensor_scalar(adg[0:64, 0:64], idq, WC(10), None, ALU.mult)
            D_()
            v.tensor_scalar(adg[0:64, 64:128], idq, WC(37), None, ALU.mult)
            D_()
            v.tensor_scalar(adg[64:128, 0:64], idq, WC(11), None, ALU.mult)
            D_()
            I(v.tensor_scalar(adg[64:128, 64:128], idq, WC(10), None, ALU.mult), "sv")
        DVE(dve_wood3)
        bank_last[3] = M("sv")
        MK2["wd3"] = M("sv")

        def pe_rows(t):
            WV(t, "sv", MK2["wd3"])
            wb(t, 2)
            ii = None
            for k, c in enumerate([38, 39, 41, 38, 40, 41, 42, 43]):
                ii = t.transpose(pb[2][0:1, 64 * k:64 * k + 64], wd[:, c:c + 1], sb["idn"][0:64, 0:64])
            I(ii, "sp")
        PE(pe_rows)
        MK2["rows"] = M("sp")

        def dve_rows(v):
            v.wait_ge(sems["sp"][0], MK2["rows"])
            I(v.tensor_copy(sb["rows6"][:, :], pb[2][0:1, 0:512]), "sv")
        DVE(dve_rows)
        bank_last[2] = M("sv")
        MK2["rows_v"] = M("sv")

        R6 = lambda a, n=128: sb["rows6"][0:1, a:a + n]

        def pe_arq(t):
            WV(t, "sv", MK2["rows_v"])
            wb(t, 3)
            t.matmul(pb[3][:, 0:128], sb["idn"][:, :], SLOT(S_ADG), start=True, stop=False)
            t.matmul(pb[3][:, 0:64], R6(0), R6(384, 64), start=False, stop=False)
            t.matmul(pb[3][:, 0:64], R6(128), R6(448, 64), start=False, stop=False)
            t.matmul(pb[3][:, 64:128], R6(128), R6(384, 64), start=False, stop=False)
            I(t.matmul(pb[3][:, 64:128], R6(256), R6(448, 64), start=False, stop=True), "sp")
        PE(pe_arq)
        MK2["arq"] = M("sp")

        def dve_arq(v):
            v.wait_ge(sems["sp"][0], MK2["arq"])
            v.tensor_copy(SLOT(S_AR), pb[3][:, 0:128])
            v.drain()
            I(v.tensor_mul(SLOT(SGNM[1]), sb["sgn"][:, :], SLOT(S_AR)), "sv")
        DVE(dve_arq)
        bank_last[3] = M("sv")
        MK2["ar_v"] = M("sv")

        def pe_art(t):
            WV(t, "sv", MK2["ar_v"])
            wb(t, 2)
            I(t.transpose(pb[2][:, 0:128], SLOT(S_AR), sb["idn"][:, :]), "sp")
        PE(pe_art)
        MK2["art"] = M("sp")

        def dve_art(v):
            v.wait_ge(sems["sp"][0], MK2["art"])
            I(v.tensor_copy(SLOT(TKEEP[1]), pb[2][:, 0:128]), "sv")
        DVE(dve_art)
        bank_last[2] = M("sv")

        emit_build(0, 0)

        def act_setup2(a):
            WV(a, "sw1", 16)
            I(a.activation(b_bf[:, :], bstage, AF.Copy), "sa")
            MK2["castB"] = M("sa")
            WV(a, "sw2", 16)
            I(a.activation(wr_bf[:, :], wrstage, AF.Copy), "sa")
        ACT(act_setup2)
        MK[("setup_sa",)] = M("sa")

        sb_rot = [0]

        def setup_bank():
            b = 2 + (sb_rot[0] % 2)
            sb_rot[0] += 1
            return b

        chain_state = {"M": S_AR, "T": TKEEP[1]}

        def emit_chain(lev):
            w2 = 2 << lev
            Msrc, Tsrc = chain_state["M"], chain_state["T"]
            do_sq = lev <= 20
            Mdst = S_MROT[lev % 2]
            if w2 == 256:
                Tslot = None
                Tdst_ap = sb["a128t"][:, :]
            else:
                Tslot = TKEEP.get(w2, S_TROT[lev % 2])
                Tdst_ap = SLOT(Tslot)
            b = setup_bank()

            def pe_ch(t, lev=lev, Msrc=Msrc, Tsrc=Tsrc, b=b, do_sq=do_sq):
                W(t, "sv")
                if do_sq:
                    t.matmul(pb[b][:, 0:128], SLOT(Tsrc) if Tsrc != "a128t" else sb["a128t"][:, :],
                             SLOT(Msrc), start=True, stop=True)
                    ii = t.matmul(pb[b][:, 128:256], SLOT(Msrc),
                                  SLOT(Tsrc) if Tsrc != "a128t" else sb["a128t"][:, :],
                                  start=True, stop=True)
                if lev >= 12:
                    t.matmul(pb[b][:, 256:257],
                             SLOT(Tsrc) if Tsrc != "a128t" else sb["a128t"][:, :],
                             sb["bprcol"][:, :], start=True, stop=False)
                    ii = t.matmul(pb[b][:, 256:257], sb["idn"][:, :], sb["bprcol"][:, :],
                                  start=False, stop=True)
                I(ii, "sp")
            PE(pe_ch)
            mm = M("sp")

            def dve_ch(v, lev=lev, b=b, Mdst=Mdst, Tdst_ap=Tdst_ap, w2=w2, mm=mm, do_sq=do_sq):
                v.wait_ge(sems["sp"][0], mm)
                ops = []
                if do_sq:
                    ops.append((SLOT(Mdst), pb[b][:, 0:128]))
                    ops.append((Tdst_ap, pb[b][:, 128:256]))
                if lev >= 12:
                    ops.append((sb["bprcol"][:, :], pb[b][:, 256:257]))
                if do_sq and w2 in SGNM:
                    for d, s in ops:
                        v.tensor_copy(d, s)
                    v.drain()
                    I(v.tensor_mul(SLOT(SGNM[w2]), sb["sgn"][:, :], SLOT(Mdst)), "sv")
                else:
                    for d, s in ops[:-1]:
                        v.tensor_copy(d, s)
                    I(v.tensor_copy(*ops[-1]), "sv")
            DVE(dve_ch)
            bank_last[b] = M("sv")
            chain_state["M"] = Mdst
            chain_state["T"] = "a128t" if w2 == 256 else Tslot

        def emit_u0():
            b = setup_bank()

            def pe_u(t, b=b):
                W(t, "sv")
                wb(t, b)
                I(t.matmul(pb[b][:, 0:1], SLOT(SGNM[1]), sb["c2col"][:, :], start=True, stop=True), "sp")
            PE(pe_u)
            mm = M("sp")

            def dve_u(v, b=b, mm=mm):
                v.wait_ge(sems["sp"][0], mm)
                I(v.tensor_copy(SLOT(S_U, 256)[:, 0:1], pb[b][:, 0:1]), "sv")
            DVE(dve_u)
            bank_last[b] = M("sv")

        def emit_udbl(w):
            b = setup_bank()

            def pe_u(t, b=b, w=w):
                W(t, "sv")
                wb(t, b)
                I(t.matmul(pb[b][:, 0:w], SLOT(SGNM[w]), SLOT(S_U, 256)[:, 0:w], start=True, stop=True), "sp")
            PE(pe_u)
            mm = M("sp")

            def dve_u(v, b=b, w=w, mm=mm):
                v.wait_ge(sems["sp"][0], mm)
                I(v.tensor_copy(SLOT(S_U, 256)[:, w:2 * w], pb[b][:, 0:w]), "sv")
            DVE(dve_u)
            bank_last[b] = M("sv")

        emit_chain(0); emit_chain(1); emit_chain(2)
        emit_build(0, 1)
        emit_chain(3); emit_chain(4); emit_chain(5)
        emit_build(1, 0)
        emit_chain(6)
        emit_u0()
        emit_chain(7)
        emit_udbl(1)
        emit_stats(0, xT[0])
        emit_chain(8)
        emit_udbl(2)
        emit_chain(9)
        emit_udbl(4)
        emit_ln("ln1", 0)
        emit_chain(10)
        emit_udbl(8)
        emit_chain(11)
        emit_udbl(16)
        emit_build(1, 1)
        emit_chain(12)
        emit_udbl(32)
        emit_chain(13)
        emit_udbl(64)
        emit_chain(14)
        emit_udbl(128)
        emit_stats(1, xT[1])
        emit_chain(15); emit_chain(16); emit_chain(17)
        emit_ln("ln1", 1)
        emit_chain(18); emit_chain(19); emit_chain(20); emit_chain(21)

        def dve_ecast(v):
            W(v, "sv")
            v.tensor_copy(sb["e_bfm"][0:64, :], SLOT(S_U, 256)[0:64, :])
            v.tensor_scalar(sb["e_bfm"][64:128, :], SLOT(S_U, 256)[64:128, :], -1.0, None, ALU.mult)
            v.tensor_copy(sb["bprjcol"][0:64, :], sb["bprcol"][0:64, :])
            v.tensor_scalar(sb["bprjcol"][64:128, :], sb["bprcol"][64:128, :], -1.0, None, ALU.mult)
            I(v.tensor_copy(SLOT(S_V, 256)[:, 0:1], sb["bprcol"][:, :]), "sv")
        DVE(dve_ecast)
        MK2["ecast"] = M("sv")

        for w in [1, 2, 4, 8, 16, 32, 64, 128]:
            b = setup_bank()

            def pe_v(t, b=b, w=w):
                W(t, "sv")
                wb(t, b)
                I(t.matmul(pb[b][:, 0:w], SLOT(TKEEP[w]), SLOT(S_V, 256)[:, 0:w], start=True, stop=True), "sp")
            PE(pe_v)
            mm = M("sp")

            def dve_v(v, b=b, w=w, mm=mm):
                v.wait_ge(sems["sp"][0], mm)
                I(v.tensor_copy(SLOT(S_V, 256)[:, w:2 * w], pb[b][:, 0:w]), "sv")
            DVE(dve_v)
            bank_last[b] = M("sv")

        for h, pdst in ((0, "p1_bfm"), (1, "p0_bfm")):
            b_vt = setup_bank()

            def pe_vt(t, h=h, b_vt=b_vt):
                W(t, "sv")
                wb(t, b_vt)
                I(t.transpose(pb[b_vt][:, 0:128], SLOT(S_V, 256)[:, 128 * h:128 * h + 128], sb["idn"][:, :]), "sp")
            PE(pe_vt)
            mm_vt = M("sp")

            def dve_vt(v, b_vt=b_vt, mm_vt=mm_vt):
                v.wait_ge(sems["sp"][0], mm_vt)
                WV(v, "sa", MK[("setup_sa",)])
                I(v.tensor_copy(SLOT(S_VTH), pb[b_vt][:, 0:128]), "sv")
            DVE(dve_vt)
            bank_last[b_vt] = M("sv")

            b_p = setup_bank()

            def pe_p(t, b_p=b_p):
                W(t, "sv")
                wb(t, b_p)
                I(t.matmul(pb[b_p][:, 0:128], sb["flip32"][:, :], SLOT(S_VTH), start=True, stop=True), "sp")
            PE(pe_p)
            mm_p = M("sp")

            def dve_p(v, b_p=b_p, pdst=pdst, mm_p=mm_p):
                v.wait_ge(sems["sp"][0], mm_p)
                I(v.tensor_copy(sb[pdst][:, :], pb[b_p][:, 0:128]), "sv")
            DVE(dve_p)
            bank_last[b_p] = M("sv")

        b_k = setup_bank()

        def pe_k(t):
            W(t, "sv")
            wb(t, b_k)
            t.matmul(pb[b_k][:, 0:1], SLOT(S_U, 256)[:, 0:128], sb["bprjcol"][:, :], start=True, stop=True)
            t.matmul(pb[b_k][:, 1:2], SLOT(S_U, 256)[:, 128:256], sb["bprjcol"][:, :], start=True, stop=True)
            I(t.matmul(pb[b_k][0:1, 2:3], sb["c3col"][:, :], sb["bprcol"][:, :], start=True, stop=True), "sp")
        PE(pe_k)
        MK2["kk"] = M("sp")

        def dve_k(v):
            v.wait_ge(sems["sp"][0], MK2["kk"])
            v.tensor_copy(sb["kcol2"][:, :], pb[b_k][:, 0:2])
            I(v.tensor_copy(sb["klineb"][0:1, 127:128], pb[b_k][0:1, 2:3]), "sv")
        DVE(dve_k)
        MK2["kv"] = M("sv")

        b_k2 = setup_bank()

        def pe_k2(t):
            WV(t, "sv", MK2["kv"])
            wb(t, b_k2)
            t.matmul(pb[b_k2][0:1, 0:128], sb["kcol2"][:, 0:1], sb["idn"][:, :], start=True, stop=True)
            I(t.matmul(pb[b_k2][0:1, 128:256], sb["kcol2"][:, 1:2], sb["idn"][:, :], start=True, stop=True), "sp")
        PE(pe_k2)
        MK2["k2"] = M("sp")

        def dve_k2(v):
            v.wait_ge(sems["sp"][0], MK2["k2"])
            I(v.tensor_copy(sb["klineb"][0:1, 128:384], pb[b_k2][0:1, 0:256]), "sv")
        DVE(dve_k2)
        bank_last[b_k] = M("sv")
        bank_last[b_k2] = M("sv")
        MK2["kline"] = M("sv")

        def pl_kscr(g):
            WV(g, "sv", MK2["kline"])
            dst = kscr.ap()[0:512].rearrange("(p f) -> p f", p=1)
            I(g.dma_start(out=dst, in_=sb["klineb"][:, :]), "sk", 16)
        PL(pl_kscr)

        def pl_hk(g):
            WV(g, "sk", 16)
            src = bass.AP(tensor=kscr.ap().tensor, offset=0, ap=[[1, 128], [1, 256]])
            I(g.dma_start(out=sb["hk"][:, :], in_=src), "sk", 16)
        PL(pl_hk)

        b_t0 = setup_bank()

        def pe_t0(t):
            WV(t, "sk", 32)
            wb(t, b_t0)
            I(t.matmul(pb[b_t0][:, 0:256], sb["flip"][:, :], sb["hk"][:, :], start=True, stop=True), "sp")
        PE(pe_t0)
        MK2["t0"] = M("sp")

        def dve_t0(v):
            v.wait_ge(sems["sp"][0], MK2["t0"])
            I(v.tensor_copy(sb["t01_bfm"][:, :], pb[b_t0][:, 0:256]), "sv")
        DVE(dve_t0)
        bank_last[b_t0] = M("sv")

        b_s1 = setup_bank()

        def pe_s1(t):
            W(t, "sv")
            wb(t, b_s1)
            t.matmul(pb[b_s1][:, 0:1], sb["p0_bfm"][:, :], sb["onesb"][:, :], start=True, stop=False)
            I(t.matmul(pb[b_s1][:, 0:1], sb["p1_bfm"][:, :], sb["onesb"][:, :], start=False, stop=True), "sp")
        PE(pe_s1)
        mm_s1 = M("sp")

        def dve_s1(v):
            v.wait_ge(sems["sp"][0], mm_s1)
            I(v.tensor_copy(SLOT(S_CM)[:, 0:1], pb[b_s1][:, 0:1]), "sv")
        DVE(dve_s1)
        bank_last[b_s1] = M("sv")

        for w in [1, 2, 4, 8]:
            b = setup_bank()

            def pe_cm(t, b=b, w=w):
                W(t, "sv")
                wb(t, b)
                lhs = sb["a128t"][:, :] if w == 1 else SLOT(TKEEP[256 * w])
                I(t.matmul(pb[b][:, 0:w], lhs, SLOT(S_CM)[:, 0:w], start=True, stop=True), "sp")
            PE(pe_cm)
            mm = M("sp")

            def dve_cm(v, b=b, w=w, mm=mm):
                v.wait_ge(sems["sp"][0], mm)
                I(v.tensor_copy(SLOT(S_CM)[:, w:2 * w], pb[b][:, 0:w]), "sv")
            DVE(dve_cm)
            bank_last[b] = M("sv")

        b_cmt = setup_bank()

        def pe_cmt(t):
            W(t, "sv")
            wb(t, b_cmt)
            I(t.transpose(pb[b_cmt][0:16, 0:128], SLOT(S_CM)[:, 0:16], sb["idn"][:, :]), "sp")
        PE(pe_cmt)
        MK2["cmt"] = M("sp")

        def dve_cmt(v):
            v.wait_ge(sems["sp"][0], MK2["cmt"])
            I(v.tensor_copy(SLOT(S_CMT)[0:16, 0:128], pb[b_cmt][0:16, 0:128]), "sv")
        DVE(dve_cmt)
        bank_last[b_cmt] = M("sv")

        b_ron = setup_bank()

        def pe_ron(t):
            W(t, "sv")
            wb(t, b_ron)
            I(t.matmul(pb[b_ron][:, 0:16], SLOT(S_CMT)[0:16, 0:128], sb["ut16"][:, :], start=True, stop=True), "sp")
        PE(pe_ron)
        MK2["ron"] = M("sp")

        def dve_ron(v):
            v.wait_ge(sems["sp"][0], MK2["ron"])
            I(v.tensor_copy(sb["ronb"][:, :], pb[b_ron][:, 0:16]), "sv")
        DVE(dve_ron)
        bank_last[b_ron] = M("sv")

        b_tc = setup_bank()

        def pe_t0cs(t):
            W(t, "sv")
            wb(t, b_tc)
            t.matmul(pb[b_tc][:, 0:1], sb["t01_bfm"][:, 0:128], sb["onesb"][:, :], start=True, stop=True)
            I(t.matmul(pb[b_tc][:, 1:2], sb["t01_bfm"][:, 128:256], sb["onesb"][:, :], start=True, stop=True), "sp")
        PE(pe_t0cs)
        mm_tc = M("sp")

        def dve_t0cs(v):
            v.wait_ge(sems["sp"][0], mm_tc)
            I(v.tensor_copy(sb["t01cscol"][:, :], pb[b_tc][:, 0:2]), "sv")
        DVE(dve_t0cs)
        bank_last[b_tc] = M("sv")
        MK2["t0cs"] = M("sv")

        b_tcr = setup_bank()

        def pe_t0csr(t):
            WV(t, "sv", MK2["t0cs"])
            wb(t, b_tcr)
            t.matmul(pb[b_tcr][0:1, 0:128], sb["t01cscol"][:, 0:1], sb["idn"][:, :], start=True, stop=True)
            I(t.matmul(pb[b_tcr][0:1, 128:256], sb["t01cscol"][:, 1:2], sb["idn"][:, :], start=True, stop=True), "sp")
        PE(pe_t0csr)
        mm_tcr = M("sp")

        def dve_t0csr(v):
            v.wait_ge(sems["sp"][0], mm_tcr)
            I(v.tensor_copy(sb["t01csrow"][:, :], pb[b_tcr][0:1, 0:256]), "sv")
        DVE(dve_t0csr)
        bank_last[b_tcr] = M("sv")

        b_tx = setup_bank()

        def pe_tcross(t):
            W(t, "sv")
            wb(t, b_tx)
            o16 = sb["ones1x128"][0:1, 0:16]
            t.matmul(pb[b_tx][0:16, 0:256], sb["ronb"][:, :], sb["e_bfm"][:, :], start=True, stop=False)
            t.matmul(pb[b_tx][0:16, 0:256], o16, sb["t01csrow"][:, :], start=False, stop=False)
            t.matmul(pb[b_tx][0:16, 128:256], o16, sb["t01csrow"][0:1, 0:128], start=False, stop=False)
            I(t.matmul(pb[b_tx][0:16, 0:256], o16, sb["drow"][:, :], start=False, stop=True), "sp")
        PE(pe_tcross)
        MK2["tx"] = M("sp")

        def dve_tcross(v):
            v.wait_ge(sems["sp"][0], MK2["tx"])
            I(v.tensor_copy(sb["tddgt"][:, :], pb[b_tx][0:16, 0:256]), "sv")
        DVE(dve_tcross)
        bank_last[b_tx] = M("sv")

        def pl_tdd(g):
            W(g, "sv")
            dst = sb["tdd"][0:1, :].rearrange("p (j c) -> p j c", j=16)
            I(g.dma_start(out=dst, in_=sb["tddgt"][:, :]), "sg", 16)
            MK2["tddmk"] = M("sg")
        PL(pl_tdd)

        def pe_wcs(t):
            W(t, "sg")
            W(t, "sa")
            wb(t, 3)
            for j in range(NCH):
                t.matmul(pb[3][:, 0:1], a_bf[:, 128 * j:128 * (j + 1)], sb["onesb"][:, :],
                         start=(j == 0), stop=(j == NCH - 1))
            ii = None
            for j in range(NCH):
                ii = t.matmul(pb[3][0:8, 2:3], wr_bf[:, 8 * j:8 * (j + 1)], sb["onesb"][:, :],
                              start=(j == 0), stop=(j == NCH - 1))
            I(ii, "sp")
        PE(pe_wcs)

        def dve_wcs(v):
            W(v, "sp")
            v.tensor_copy(sb["wk"][:, 0:1], pb[3][:, 0:1])
            I(v.tensor_copy(sb["wk"][0:8, 1:2], pb[3][0:8, 2:3]), "sv")
        DVE(dve_wcs)
        bank_last[3] = M("sv")

        def pe_wcs2(t):
            W(t, "sv")
            wb(t, 4)
            t.transpose(pb[4][0:1, 0:128], sb["wk"][:, 0:1], sb["idn"][:, :])
            I(t.transpose(pb[4][0:1, 128:136], sb["wk"][0:8, 1:2], sb["idn"][0:8, 0:8]), "sp")
        PE(pe_wcs2)

        def dve_wcs2(v):
            W(v, "sp")
            v.tensor_scalar(sb["anegrow"][:, :], pb[4][0:1, 0:128], -1.0, None, ALU.mult)
            I(v.tensor_scalar(sb["wrnegrow"][:, :], pb[4][0:1, 128:136], -1.0, None, ALU.mult), "sv")
        DVE(dve_wcs2)
        bank_last[4] = M("sv")
        MK2["convready"] = M("sv")

        for k in range(NT // 2):
            tA, tB = 2 * k, 2 * k + 1

            if k >= 1:
                for g in range(NG):
                    emit_build(tA, g)
                emit_stats(tA, xT[0])
                emit_ln("ln1", 0)
                for g in range(NG):
                    emit_build(tB, g)
                emit_stats(tB, xT[1])
                emit_ln("ln1", 1)

            emit_lnT("ln1", 0)
            emit_lnT("ln1", 1)

            NC2 = NCH // 2
            epimark = {}
            epi_pend = []
            epi_done = []
            for r2 in range(NC2):
                q, subq = divmod(r2, 2)
                ckmms = {}
                for par, t_i in ((0, tA), (1, tB)):
                    obk = (0, 1) if par == 0 else (2, 3)
                    sbank = 4 + par
                    zfs, zbs = sb["zf" + str(par)], sb["zb" + str(par)]

                    def pe_ck(t, t_i=t_i, par=par, r2=r2, q=q, subq=subq, obk=obk,
                              sbank=sbank, zfs=zfs, zbs=zbs):
                        if r2 == 0 and t_i == 0:
                            WV(t, "sv", MK2["convready"])
                            WV(t, "sg", MK2["tddmk"])
                        if subq == 0:
                            wb(t, *obk)
                            WV(t, "sv", MK[("negmb", par)])
                            for g in range(NG):
                                t.matmul(pb[obk[g]][:, 0:512],
                                         sb["negmb" + str(par)][0:1, 128 * g:128 * g + 128],
                                         sb["tdd"][0:1, 512 * q:512 * q + 512], start=True, stop=False)
                        ii = None
                        c0 = RT * (2 * r2)
                        c1 = RT * (2 * r2 + 1)
                        if r2 > 0:
                            WV(t, "sv", MK[("zev", t_i, r2 - 1)])
                            for g in range(NG):
                                ii = t.matmul(pb[obk[g]][:, 256 * subq:256 * subq + 256],
                                              zbs[:, 128 * g:128 * g + 128], sb["e_bfm"][:, :],
                                              start=False, stop=False)
                        for g in range(NG):
                            t.matmul(pb[obk[g]][:, 256 * subq:256 * subq + 256],
                                     xT[par][:, c0 + 128 * g:c0 + 128 * g + 128],
                                     sb["t01_bfm"][:, :], start=False, stop=False)
                            ii = t.matmul(pb[obk[g]][:, 256 * subq + 128:256 * subq + 256],
                                          xT[par][:, c1 + 128 * g:c1 + 128 * g + 128],
                                          sb["t01_bfm"][:, 0:128], start=False,
                                          stop=(subq == 1))
                        if r2 < NC2 - 1:
                            wb(t, sbank)
                            if r2 == 0:
                                t.matmul(pb[sbank][:, 0:RT], sb["p0_bfm"][:, :],
                                         xT[par][:, c0:c0 + RT], start=True, stop=False)
                                ii = t.matmul(pb[sbank][:, 0:RT], sb["p1_bfm"][:, :],
                                              xT[par][:, c1:c1 + RT], start=False, stop=True)
                            else:
                                t.matmul(pb[sbank][:, 0:RT], sb["p0_bfm"][:, :],
                                         xT[par][:, c0:c0 + RT], start=True, stop=False)
                                t.matmul(pb[sbank][:, 0:RT], sb["p1_bfm"][:, :],
                                         xT[par][:, c1:c1 + RT], start=False, stop=False)
                                ii = t.matmul(pb[sbank][:, 0:RT], sb["a128t"][:, :], zfs[:, :],
                                              start=False, stop=True)
                        I(ii, "sp")
                    PE(pe_ck)
                    ckmms[par] = M("sp")

                    if r2 < NC2 - 1:
                        def dve_z(v, ckmm=ckmms[par], t_i=t_i, r2=r2, sbank=sbank, zfs=zfs, zbs=zbs):
                            v.wait_ge(sems["sp"][0], ckmm)
                            v.tensor_copy(zfs[:, :], pb[sbank][:, 0:RT])
                            I(v.tensor_copy(zbs[:, :], pb[sbank][:, 0:RT]), "sv")
                        DVE(dve_z)
                        MK[("zev", t_i, r2)] = M("sv")
                        bank_last[sbank] = M("sv")

                if r2 == 0:
                    def act_prem(a):
                        WV(a, "sv", MK[("lnC", 0)])
                        WV(a, "sv", MK[("lnC", 1)])
                        for qq in range(8):
                            for par in range(2):
                                for g in range(NG):
                                    ps = x[par][:, DM * g + 512 * qq:DM * g + 512 * qq + 512]
                                    I(a.activation(ps, ps, AF.Copy, scale=sb["sc1c" + str(par)][:, g:g + 1]), "sa")
                                    MK[("prem", par, g, qq)] = M("sa")
                    ACT(act_prem)

                if subq == 1:
                    for par, t_i in ((0, tA), (1, tB)):
                        obk = (0, 1) if par == 0 else (2, 3)
                        epi_pend.append((par, t_i, q, obk, ckmms[par]))
                    epi_done = epi_pend
                    epi_pend = []
                    for par, t_i, q_, obk, ckmm in epi_done:
                        def dve_epi(v, t_i=t_i, q=q_, obk=obk, ckmm=ckmm, par=par):
                            v.wait_ge(sems["sp"][0], ckmm)
                            for g in range(NG):
                                WV(v, "sa", MK[("prem", par, g, q)])
                                rng = slice(DM * g + 512 * q, DM * g + 512 * q + 512)
                                I(v.scalar_tensor_tensor(x[par][:, rng], pb[obk[g]][:, 0:512],
                                                         sb["invsc" + str(par)][:, g:g + 1], x[par][:, rng],
                                                         ALU.mult, ALU.add), "sv")
                        DVE(dve_epi)
                        for b_ in obk:
                            bank_last[b_] = M("sv")
                        epimark[t_i] = M("sv")
                    epi_done = []

            for par, t_i, q_, obk, ckmm in epi_done + epi_pend:
                def dve_epi(v, t_i=t_i, q=q_, obk=obk, ckmm=ckmm, par=par):
                    v.wait_ge(sems["sp"][0], ckmm)
                    for g in range(NG):
                        WV(v, "sa", MK[("prem", par, g, q)])
                        rng = slice(DM * g + 512 * q, DM * g + 512 * q + 512)
                        I(v.scalar_tensor_tensor(x[par][:, rng], pb[obk[g]][:, 0:512],
                                                 sb["invsc" + str(par)][:, g:g + 1], x[par][:, rng],
                                                 ALU.mult, ALU.add), "sv")
                DVE(dve_epi)
                for b_ in obk:
                    bank_last[b_] = M("sv")
                epimark[t_i] = M("sv")

            for par, t_i in ((0, tA), (1, tB)):
                s_ = lambda nm, par=par: sb[nm + str(par)]
                for g in range(NG):
                    MK[("epi", t_i, g)] = epimark[t_i]
                def act_cast2(a, t_i=t_i, par=par):
                    for g in range(NG):
                        slot = 2 * par + (g % 2)
                        WV(a, "sv", MK[("epi", t_i, g)])
                        WV(a, "sp", MK.get(("slotfree", slot), 0))
                        I(a.activation(xb[slot][:, :], x[par][:, DM * g:DM * (g + 1)], AF.Copy), "sa")
                        MK[("cast2", t_i, g)] = M("sa")
                ACT(act_cast2)
                for g in range(NG):
                    slot = 2 * par + (g % 2)
                    emit_tr(t_i, g, xTn, ("cast2", t_i, g), slot)
                emit_stats(t_i, xTn)
                emit_ln("ln2", par)

                def pe_router(t, par=par):
                    wb(t, 1)
                    ii = None
                    for j in range(NCH):
                        ii = t.matmul(pb[1][0:8, 0:RT], wr_bf[:, 8 * j:8 * (j + 1)],
                                      xTn[:, RT * j:RT * j + RT], start=(j == 0), stop=False)
                    WV(t, "sv", MK[("negmb", par)])
                    I(t.matmul(pb[1][0:8, 0:RT], sb["wrnegrow"][:, :],
                               sb["posmb" + str(par)][0:1, 0:RT], start=False, stop=True), "sp")
                PE(pe_router)

                def dve_raw8(v):
                    W(v, "sp")
                    I(v.tensor_copy(sb["raw8"][:, :], pb[1][0:8, 0:RT]), "sv")
                DVE(dve_raw8)
                bank_last[1] = M("sv")
                MK[("raw8",)] = M("sv")

                def pe_lt(t):
                    WV(t, "sv", MK[("raw8",)])
                    wb(t, 5)
                    ii = None
                    for g in range(NG):
                        ii = t.transpose(pb[5][:, 8 * g:8 * g + 8], sb["raw8"][0:8, 128 * g:128 * (g + 1)], sb["idn"][0:8, 0:8])
                    I(ii, "sp")
                PE(pe_lt)
                ltmm = M("sp")

                emit_lnT("ln2", par)

                def pe_low(t, par=par):
                    wb(t, 0)
                    for j in range(NCH):
                        t.matmul(pb[0][:, 0:RT], a_bf[:, 128 * j:128 * (j + 1)],
                                 xTn[:, RT * j:RT * j + RT], start=(j == 0), stop=False)
                    I(t.matmul(pb[0][:, 0:RT], sb["anegrow"][:, :],
                               sb["posmb" + str(par)][0:1, 0:RT], start=False, stop=True), "sp")
                PE(pe_low)

                def dve_sm1(v, ltmm=ltmm, par=par):
                    v.wait_ge(sems["sp"][0], ltmm)
                    WV(v, "sv", MK[("lnC", par)])
                    v.tensor_copy(sb["scrt"][:, :], pb[5][:, 0:8 * NG])
                    v.drain()
                    for g in range(NG):
                        v.tensor_scalar(sb["rawt"][:, 8 * g:8 * (g + 1)], sb["scrt"][:, 8 * g:8 * (g + 1)],
                                        sb["invsc" + str(par)][:, g:g + 1], None, ALU.mult)
                    v.drain()
                    for g in range(NG):
                        v.tensor_add(sb["scrt"][:, 8 * g:8 * (g + 1)], sb["rawt"][:, 8 * g:8 * (g + 1)], sb["brrep"][:, :])
                    v.drain()
                    for g in range(NG):
                        v.reduce_max(sb["smx"][:, g:g + 1], sb["scrt"][:, 8 * g:8 * (g + 1)], axis=AX.X)
                    v.drain()
                    ii = None
                    for g in range(NG):
                        ii = v.tensor_scalar(sb["rawt"][:, 8 * g:8 * (g + 1)], sb["scrt"][:, 8 * g:8 * (g + 1)],
                                             sb["smx"][:, g:g + 1], None, ALU.subtract)
                    I(ii, "sv")
                DVE(dve_sm1)
                bank_last[5] = M("sv")

                def act_sm(a):
                    W(a, "sv")
                    I(a.activation(sb["scrt"][:, :], sb["rawt"][:, :], AF.Exp), "sa")
                ACT(act_sm)

                def dve_sm2(v):
                    W(v, "sa")
                    for g in range(NG):
                        v.reduce_sum(sb["ssum"][:, g:g + 1], sb["scrt"][:, 8 * g:8 * (g + 1)], axis=AX.X)
                    v.drain()
                    v.reciprocal(sb["srec"][:, :], sb["ssum"][:, :])
                    v.drain()
                    for g in range(NG):
                        v.tensor_scalar(sb["rawt"][:, 8 * g:8 * (g + 1)], sb["scrt"][:, 8 * g:8 * (g + 1)],
                                        sb["srec"][:, g:g + 1], 8.0, ALU.mult, ALU.mult)
                    v.drain()
                    I(v.tensor_copy(sb["scrt"][:, :], sb["rawt"][:, :]), "sv")
                DVE(dve_sm2)
                MK[("scores",)] = M("sv")

                def pe_st2(t):
                    WV(t, "sv", MK[("scores",)])
                    wb(t, 3)
                    ii = None
                    for g in range(NG):
                        ii = t.transpose(pb[3][0:8, 128 * g:128 * (g + 1)], sb["scrt"][:, 8 * g:8 * (g + 1)], sb["idn"][:, :])
                    I(ii, "sp")
                PE(pe_st2)

                def dve_scfm(v):
                    W(v, "sp")
                    I(v.tensor_copy(sb["scfm"][:, :], pb[3][0:8, 0:RT]), "sv")
                DVE(dve_scfm)
                bank_last[3] = M("sv")
                MK[("scfm",)] = M("sv")

                def pe_emm(t):
                    WV(t, "sv", MK[("scfm",)])
                    wb(t, 4)
                    I(t.matmul(pb[4][:, 0:RT], sb["e8"][:, :], sb["scfm"][:, :], start=True, stop=True), "sp")
                PE(pe_emm)

                def dve_lows(v):
                    W(v, "sp")
                    v.tensor_copy(sb["se"][:, :], pb[4][:, 0:RT])
                    v.drain()
                    I(v.tensor_mul(sb["lows"][:, :], pb[0][:, 0:RT], sb["se"][:, :]), "sv")
                DVE(dve_lows)
                bank_last[4] = M("sv")
                bank_last[0] = M("sv")
                MK[("lows",)] = M("sv")

                for g in range(NG):
                    for o in range(8):
                        def pe_op(t, g=g, o=o):
                            if o == 0 and g == 0:
                                WV(t, "sv", MK[("lows",)])
                            wb(t, 2 + (o % 2))
                            bk = pb[2 + (o % 2)]
                            I(t.matmul(bk[:, 0:512], sb["lows"][:, 128 * g:128 * (g + 1)],
                                       b_bf[:, 512 * o:512 * o + 512], start=True, stop=True), "sp")
                        PE(pe_op)
                        opmm = M("sp")

                        def dve_fin(v, g=g, o=o, opmm=opmm, par=par):
                            v.wait_ge(sems["sp"][0], opmm)
                            bk = pb[2 + (o % 2)]
                            rng = slice(DM * g + 512 * o, DM * g + 512 * o + 512)
                            I(v.scalar_tensor_tensor(x[par][:, rng], bk[:, 0:512],
                                                     sb["sc2c" + str(par)][:, g:g + 1],
                                                     x[par][:, rng], ALU.mult, ALU.add), "sv")
                        DVE(dve_fin)
                        bank_last[2 + (o % 2)] = M("sv")
                    finmark = M("sv")

                    def sp_store(sp, g=g, t_i=t_i, par=par, finmark=finmark):
                        WV(sp, "sv", finmark)
                        rows = t_i * RT + g * 128
                        I(sp.dma_start(out=y[rows:rows + 128, :], in_=x[par][:, DM * g:DM * (g + 1)]), "st", 16)
                    SP(sp_store)
                st_marks[t_i] = M("st")
                if t_i + 2 < NT:
                    emit_load(t_i + 2)

        def sp_fin(sp):
            W(sp, "st")
        SP(sp_fin)

    return nc


_NC = None


def _get_nc():
    global _NC
    if _NC is None:
        _NC = build_nc()
    return _NC


def kernel(L0_attn_u, L0_attn_Lambda_ri, L0_attn_p_ri, L0_attn_q_ri, L0_attn_Vc_ri,
           L0_attn_Ct_ri, L0_attn_Bp, L0_attn_D, L0_attn_log_step,
           L0_ffn_A, L0_ffn_B, L0_ffn_Wr, L0_ffn_br):
    f32 = np.float32
    u = np.ascontiguousarray(np.asarray(L0_attn_u, dtype=f32).reshape(NROWS, DM))
    lam = np.asarray(L0_attn_Lambda_ri, dtype=f32)
    p_ri = np.asarray(L0_attn_p_ri, dtype=f32)
    q_ri = np.asarray(L0_attn_q_ri, dtype=f32)
    vc = np.asarray(L0_attn_Vc_ri, dtype=f32)
    ct = np.asarray(L0_attn_Ct_ri, dtype=f32).reshape(64, 2)
    bp = np.asarray(L0_attn_Bp, dtype=f32).reshape(64, 1)
    D = np.asarray(L0_attn_D, dtype=f32).reshape(1, 1)
    lst = np.asarray(L0_attn_log_step, dtype=f32).reshape(1, 1)
    A = np.asarray(L0_ffn_A, dtype=f32)
    B = np.asarray(L0_ffn_B, dtype=f32)
    Wr = np.asarray(L0_ffn_Wr, dtype=f32)
    br = np.asarray(L0_ffn_br, dtype=f32).reshape(1, 8)

    h_A = np.ascontiguousarray(
        A.transpose(2, 0, 1).reshape(DM, 128).reshape(NCH, 128, 128).transpose(1, 0, 2).reshape(128, DM))
    h_B = np.ascontiguousarray(B.transpose(0, 2, 1).reshape(128, DM))
    h_Wr = np.ascontiguousarray(
        Wr.T.reshape(NCH, 128, 8).transpose(1, 0, 2).reshape(128, NCH * 8))

    p_vec = np.ascontiguousarray(np.concatenate([
        lam[:, 0:1], lam[:, 1:2], p_ri[:, 0:1], p_ri[:, 1:2],
        q_ri[:, 0:1], q_ri[:, 1:2], ct[:, 0:1], ct[:, 1:2], bp,
        vc[:, :, 0].T, vc[:, :, 1].T], axis=1))
    p_row = np.ascontiguousarray(np.concatenate([D, lst, br], axis=1))

    cst = _consts()
    base_map = dict(cst)
    base_map.update({
        "w_A": h_A, "w_B": h_B, "w_Wr": h_Wr,
        "p_vec": p_vec, "p_row": p_row,
    })

    in_maps = []
    for c in range(NCORES):
        m = dict(base_map)
        m["u"] = u[c * RPC:(c + 1) * RPC]
        in_maps.append(m)

    nc = _get_nc()
    res = run_bass_kernel_spmd(nc, in_maps, core_ids=list(range(NCORES)))
    out = np.concatenate([np.asarray(res.results[c]["y"]) for c in range(NCORES)], axis=0)
    return out.reshape(2, L, DM).astype(np.float32)
